# version 08
# baseline (speedup 1.0000x reference)
import numpy as np
import concourse.bass as bass
import concourse.mybir as mybir
from concourse.bass_utils import run_bass_kernel_spmd

F32 = mybir.dt.float32
BF16 = mybir.dt.bfloat16
AOT = mybir.AluOpType
AFT = mybir.ActivationFunctionType
AXX = mybir.AxisListType.X

B, S, D = 4, 2048, 2048
HID = 8192
T, NT, NKC, NCH = 1024, 8, 16, 8
EPS = 1e-5

ENGS = ("sync", "gpsimd", "tensor", "vector", "scalar")
DEBUG = False
DEBUG_BARRIER = True
BARRIERS = set()


class Bld:
    def __init__(self, nc):
        self.nc = nc
        self.q = {e: [] for e in ENGS}
        self.sems = {}
        self.cnt = {}
        names = ["SX", "SW", "ST", "SV", "SS", "SG", "TB", "XI", "XA", "XB", "XR",
                 "LT0", "LT1", "L2T", "KQ0", "KQ1", "SP", "SO0", "SO1", "AC"]
        names += [f"W{i}" for i in range(16)]
        for s in names:
            self.sems[s] = nc.alloc_semaphore(s)
            self.cnt[s] = 0

    def emit(self, eng, fn):
        self.q[eng].append(fn)

    def wait(self, eng, sem, val=None):
        v = self.cnt[sem] if val is None else val
        if v <= 0:
            return
        s = self.sems[sem]
        self.emit(eng, lambda e, s=s, v=v: e.wait_ge(s, v))

    def op(self, eng, fn, inc=None, amt=1):
        if inc is None:
            self.emit(eng, fn)
            return None
        s = self.sems[inc]
        self.cnt[inc] += amt
        self.emit(eng, lambda e, s=s, a=amt, f=fn: f(e).then_inc(s, a))
        return self.cnt[inc]

    def dma(self, eng, out_ap, in_ap, inc="SW", accum=False):
        s = self.sems[inc]
        self.cnt[inc] += 16
        if accum:
            self.emit(eng, lambda e, o=out_ap, i=in_ap, s=s:
                      e.dma_start(o, i, accum_op=AOT.add).then_inc(s, 16))
        else:
            self.emit(eng, lambda e, o=out_ap, i=in_ap, s=s:
                      e.dma_start(o, i).then_inc(s, 16))
        return self.cnt[inc]

    def dmat(self, out_ap, in_ap, inc="SX"):
        s = self.sems[inc]
        self.cnt[inc] += 16
        self.emit("sync", lambda e, o=out_ap, i=in_ap, s=s:
                  e.dma_start_transpose(o, i).then_inc(s, 16))
        return self.cnt[inc]


def AP(t, off, dims):
    h = t.tensor if hasattr(t, "tensor") else t
    return bass.AP(h, off, dims)


def build_program():
    nc = bass.Bass("TRN2", target_bir_lowering=False)

    x_own = nc.dram_tensor("x_own", [T, D], F32, kind="ExternalInput").ap()
    x_pre = nc.dram_tensor("x_pre", [T, D], F32, kind="ExternalInput").ap()
    wqT = nc.dram_tensor("wqT", [D, 2048], F32, kind="ExternalInput").ap()
    wkT = nc.dram_tensor("wkT", [D, 512], F32, kind="ExternalInput").ap()
    wvT = nc.dram_tensor("wvT", [D, 512], F32, kind="ExternalInput").ap()
    woT = nc.dram_tensor("woT", [2048, D], F32, kind="ExternalInput").ap()
    wgP = nc.dram_tensor("wgP", [64, 128, 2048], F32, kind="ExternalInput").ap()
    wuP = nc.dram_tensor("wuP", [64, 128, 2048], F32, kind="ExternalInput").ap()
    wdT = nc.dram_tensor("wdT", [HID, D], F32, kind="ExternalInput").ap()
    tbl = nc.dram_tensor("tbl", [128, 6176], F32, kind="ExternalInput").ap()
    out_hbm = nc.dram_tensor("out", [T, D], F32, kind="ExternalOutput").ap()
    x1_hbm = nc.dram_tensor("x1spill", [T, D], F32, kind="Internal").ap()

    SB = 16640
    _A = nc.alloc_sbuf_tensor_at
    A = lambda name, shape, dt, offset: _A(name, shape, dt, offset=SB + offset)
    fring = A("fring", [128, 4096], F32, offset=0)
    lnT = A("lnT", [128, 16384], BF16, offset=16384)
    gbuf = A("gbuf", [128, 16384], BF16, offset=49152)
    gbufF = A("gbufF", [128, 4096], F32, offset=49152)
    tgF = A("tgF", [128, 4096], F32, offset=65536)
    qoT = A("qoT", [128, 16384], BF16, offset=81920)
    kT = A("kT", [128, 4096], BF16, offset=114688)
    ktok = A("ktok", [128, 4096], BF16, offset=122880)
    vtok = A("vtok", [128, 4096], BF16, offset=131072)
    KVa = A("KVa", [128, 2048], F32, offset=139264)
    KVb = A("KVb", [128, 2048], F32, offset=147456)
    acc = A("acc", [128, 16384], F32, offset=81920)
    aoF1 = A("aoF1", [128, 6144], F32, offset=114688)
    stage2 = A("stage2", [128, 2048], BF16, offset=139264)
    xrl = A("xrl", [128, 2048], F32, offset=147456)
    KVbf = A("KVbf", [128, 2048], BF16, offset=155648)
    warena = A("warena", [128, 8192], BF16, offset=159744)
    ring = A("ring", [128, 4096], BF16, offset=176128)
    ringF = A("ringF", [128, 2048], F32, offset=176128)
    scos = A("scos", [128, 1024], BF16, offset=184320)
    ssin = A("ssin", [128, 1024], BF16, offset=186368)
    sm0 = 188416
    stat = A("stat", [128, 64], F32, offset=sm0)
    lamF = A("lamF", [128, 80], F32, offset=sm0 + 256)
    idxS = A("idxS", [128, 256], F32, offset=sm0 + 576)
    ktil = A("ktil", [128, 512], BF16, offset=sm0 + 1600)
    qtil = A("qtil", [128, 512], BF16, offset=sm0 + 2624)
    scm = A("scm", [128, 1024], BF16, offset=sm0 + 3648)
    dumV = A("dumV", [128, 16], F32, offset=sm0 + 5696)
    sqd = A("sqd", [128, 2048], BF16, offset=sm0 + 5888)
    scm2 = A("scm2", [128, 1024], BF16, offset=sm0 + 9984)
    dumB = A("dumB", [128, 128], F32, offset=sm0 + 12032)

    p0 = nc.alloc_psum_tensor("p0", [128, 2048], F32)
    p1 = nc.alloc_psum_tensor("p1", [128, 2048], F32)
    PS = (p0, p1)

    b = Bld(nc)

    def V(fn, inc=None):
        return b.op("vector", fn, inc=inc)

    def Sc(fn, inc=None):
        return b.op("scalar", fn, inc=inc)

    def Tt(fn, inc=None):
        return b.op("tensor", fn, inc=inc)

    def vdummy(n=2):
        for _ in range(n):
            V(lambda e: e.tensor_copy(dumB[:, 0:64], dumB[:, 64:128]))

    def vfence(inc="SV"):
        vdummy(2)
        return V(lambda e: e.tensor_copy(dumB[:, 0:64], dumB[:, 64:128]), inc=inc)

    def sfence(inc="SS"):
        Sc(lambda e: e.copy(dumV[:, 0:1], dumV[:, 4:5]))
        return Sc(lambda e: e.copy(dumV[:, 1:2], dumV[:, 5:6]), inc=inc)

    dbg_outs = []

    def _dbg(name, src_ap, shape):
        if not DEBUG:
            return
        d = nc.dram_tensor("dbg_" + name, list(shape), F32, kind="ExternalOutput").ap()
        for sname in list(b.sems):
            if sname != "SW":
                b.wait("gpsimd", sname)
        b.dma("gpsimd", d[:, :], src_ap, "SW")
        if DEBUG_BARRIER:
            for eng in ENGS:
                b.wait(eng, "SW")
        dbg_outs.append(name)

    def _bar(pid):
        if pid not in BARRIERS:
            return
        for sname in list(b.sems):
            b.wait("gpsimd", sname)
        b.op("gpsimd", lambda e: e.memset(dumV[:, 8:9], 0.0), inc="SG", amt=1)
        for eng in ENGS:
            b.wait(eng, "SW")
            b.wait(eng, "SG")

    SQD = 8192

    def tg(c0, w=1):
        return tgF[:, c0:c0 + w]

    b.op("gpsimd", lambda e: e.memset(KVa[:, :], 0.0), inc="SG", amt=1)
    b.op("gpsimd", lambda e: e.memset(stat[:, :], 0.0), inc="SG", amt=1)
    b.op("gpsimd", lambda e: e.memset(dumV[:, :], 0.0), inc="SG", amt=1)
    b.op("gpsimd", lambda e: e.memset(dumB[:, :], 0.0), inc="SG", amt=1)
    for _eng in ("vector", "scalar", "tensor", "sync"):
        b.wait(_eng, "SG")
    b.dma("gpsimd", ring[:, 0:1024], tbl[:, 4096:5120], "TB")
    b.dma("gpsimd", ring[:, 1024:2048], tbl[:, 5120:6144], "TB")
    b.dma("gpsimd", lamF[:, 32:48], tbl[:, 6144:6160], "TB")
    b.dma("gpsimd", lamF[:, 48:64], tbl[:, 6160:6176], "TB")
    tb1 = b.dma("gpsimd", scos[:, :], tbl[:, 2048:3072], "TB")
    tb1 = b.dma("gpsimd", ssin[:, :], tbl[:, 3072:4096], "TB")
    b.wait("vector", "TB", tb1)

    def ln_stats(src_tile, jc, sq_wait_sv=None):
        V(lambda e, s=src_tile, jc=jc: e.reduce_sum(stat[:, jc:jc + 1], s, AXX))
        if sq_wait_sv is not None:
            b.wait("scalar", "SV", sq_wait_sv)
        ss = Sc(lambda e, s=src_tile, jc=jc: e.activation(
            sqd[:, :], s, AFT.Square, accum_out=stat[:, 8 + jc:9 + jc]),
            inc="SS")
        b.wait("vector", "SS", ss)
        vdummy(2)
        V(lambda e, jc=jc: e.tensor_scalar_mul(stat[:, 16 + jc:17 + jc], stat[:, jc:jc + 1], 1.0 / 2048.0))
        vdummy(2)
        V(lambda e, jc=jc: e.tensor_mul(stat[:, 40 + jc:41 + jc], stat[:, 16 + jc:17 + jc], stat[:, 16 + jc:17 + jc]))
        V(lambda e, jc=jc: e.tensor_scalar_mul(stat[:, 48 + jc:49 + jc], stat[:, 8 + jc:9 + jc], 1.0 / 2048.0))
        vdummy(2)
        V(lambda e, jc=jc: e.tensor_sub(stat[:, 56 + jc:57 + jc], stat[:, 48 + jc:49 + jc], stat[:, 40 + jc:41 + jc]))
        vdummy(2)
        vv = V(lambda e, jc=jc: e.tensor_scalar_add(stat[:, 40 + jc:41 + jc], stat[:, 56 + jc:57 + jc], EPS), inc="SV")
        b.wait("scalar", "SV", vv)
        ss = Sc(lambda e, jc=jc: e.activation(stat[:, 48 + jc:49 + jc], stat[:, 40 + jc:41 + jc], AFT.Sqrt), inc="SS")
        b.wait("vector", "SS", ss)
        V(lambda e, jc=jc: e.reciprocal(stat[:, 24 + jc:25 + jc], stat[:, 48 + jc:49 + jc]))
        vdummy(2)
        V(lambda e, jc=jc: e.tensor_mul(stat[:, 56 + jc:57 + jc], stat[:, 16 + jc:17 + jc], stat[:, 24 + jc:25 + jc]))
        vdummy(2)
        return V(lambda e, jc=jc: e.tensor_scalar_mul(stat[:, 32 + jc:33 + jc], stat[:, 56 + jc:57 + jc], -1.0), inc="SV")

    def ln_apply(src_tile, dst_tile, jc, sv):
        b.wait("scalar", "SV", sv)
        Sc(lambda e, s=src_tile, d=dst_tile, jc=jc: e.activation(
            d, s, AFT.Identity, bias=stat[:, 32 + jc:33 + jc], scale=stat[:, 24 + jc:25 + jc]))
        return sfence()

    def ln_phase(x_src):
        applies = {}
        transposes = {}
        for j in range(NT):
            par = j % 2
            xsem = "XA" if par == 0 else "XB"
            ltsem = "LT0" if par == 0 else "LT1"
            slot = fring[:, par * 2048:par * 2048 + 2048]
            if j >= 2:
                b.wait("sync", "SS", applies[j - 2])
            ld = b.dma("sync", slot, x_src[j * 128:(j + 1) * 128, :], xsem)
            b.wait("vector", xsem, ld)
            b.wait("scalar", xsem, ld)
            sv = ln_stats(slot, j % 8)
            stg = gbuf[:, par * 2048:par * 2048 + 2048]
            if j >= 2:
                b.wait("scalar", ltsem, transposes[j - 2])
            a = ln_apply(slot, stg, j % 8, sv)
            applies[j] = a
            b.wait("sync", "SS", a)
            transposes[j] = b.dmat(AP(lnT, j * 128, [[16384, 128], [1024, 16], [1, 128]]), stg, inc=ltsem)

    wslot_state = {"next": 0}
    slot_free_st = [0] * 16

    def wslot():
        i = wslot_state["next"]
        wslot_state["next"] = (i + 1) % 16
        return i

    def kouter_pass(lhs_fn, w_src_fn, evac_fn, pre_waits=()):
        slots, loads = [], []
        for k in range(NKC):
            i = wslot()
            if slot_free_st[i] > 0:
                b.wait("gpsimd", "ST", slot_free_st[i])
            sl = warena[:, i * 512:(i + 1) * 512]
            slots.append((i, sl))
            loads.append(b.dma("gpsimd", sl, w_src_fn(k), f"W{i}"))
        b.wait("tensor", "SV")
        b.wait("tensor", "SS")
        for sem in pre_waits:
            b.wait("tensor", sem)
        for k in range(NKC):
            b.wait("tensor", f"W{slots[k][0]}", loads[k])
            for t in range(NT):
                reg = PS[t // 4][:, (t % 4) * 512:(t % 4) * 512 + 512]
                lhs = lhs_fn(k, t)
                if t == NT - 1:
                    st = Tt(lambda e, reg=reg, lhs=lhs, sl=slots[k][1], k=k: e.matmul(
                        reg, lhs, sl, start=(k == 0), stop=(k == NKC - 1)), inc="ST")
                    slot_free_st[slots[k][0]] = st
                else:
                    Tt(lambda e, reg=reg, lhs=lhs, sl=slots[k][1], k=k: e.matmul(
                        reg, lhs, sl, start=(k == 0), stop=(k == NKC - 1)))
        stlast = b.cnt["ST"]
        for t in range(NT):
            reg = PS[t // 4][:, (t % 4) * 512:(t % 4) * 512 + 512]
            evac_fn(t, reg, stlast)

    def ln_lhs(k, t):
        return AP(lnT, k * 1024 + t * 128, [[16384, 128], [1, 128]])

    def emit_rope(psrc, dst, t, nh):
        pT, po = psrc.tensor, psrc.offset
        dstT, do = dst.tensor, dst.offset
        cos_lo = AP(scos, t * 128, [[1024, 128], [0, nh], [1, 64]])
        sin_lo = AP(ssin, t * 128, [[1024, 128], [0, nh], [1, 64]])
        q_lo = bass.AP(pT, po, [[2048, 128], [128, nh], [1, 64]])
        q_hi = bass.AP(pT, po + 64, [[2048, 128], [128, nh], [1, 64]])
        o_lo = bass.AP(dstT, do, [[16384, 128], [128, nh], [1, 64]])
        o_hi = bass.AP(dstT, do + 64, [[16384, 128], [128, nh], [1, 64]])
        m1 = AP(tgF, 96, [[4096, 128], [64, nh], [1, 64]])
        m2 = AP(tgF, 608, [[4096, 128], [64, nh], [1, 64]])
        m3 = AP(tgF, 1120, [[4096, 128], [64, nh], [1, 64]])
        m4 = AP(tgF, 1632, [[4096, 128], [64, nh], [1, 64]])
        V(lambda e: e.tensor_mul(m1, q_lo, cos_lo))
        V(lambda e: e.tensor_mul(m2, q_hi, sin_lo))
        V(lambda e: e.tensor_mul(m3, q_hi, cos_lo))
        V(lambda e: e.tensor_mul(m4, q_lo, sin_lo))
        vdummy(2)
        V(lambda e: e.tensor_sub(o_lo, m1, m2))
        vdummy(2)
        V(lambda e: e.tensor_add(o_hi, m3, m4))
        vdummy(4)

    stage_free = {0: 0, 1: 0}

    def evac_k(to_kT):
        def f(t, reg, st):
            b.wait("vector", "ST", st)
            par = t % 2
            stB = gbuf[:, 6144 + par * 1024: 6144 + par * 1024 + 512]
            if stage_free[par] > 0:
                b.wait("vector", "KQ0" if par == 0 else "KQ1", stage_free[par])
            emit_rope(reg, stB, t, 4)
            V(lambda e, t=t, stB=stB: e.tensor_copy(
                AP(ktok, t * 512, [[4096, 128], [1, 512]]), stB))
            kv = vfence()
            if to_kT:
                b.wait("sync", "SV", kv)
                stage_free[par] = b.dmat(
                    AP(kT, t * 128, [[4096, 128], [1024, 4], [1, 128]]), stB,
                    inc="KQ0" if par == 0 else "KQ1")
        return f

    def evac_v(t, reg, st):
        b.wait("vector", "ST", st)
        V(lambda e, reg=reg, t=t: e.tensor_copy(
            AP(vtok, t * 512, [[4096, 128], [1, 512]]), reg),
          inc="SV" if t == NT - 1 else None)

    def evac_q(qq):
        def f(t, reg, st):
            b.wait("vector", "ST", st)
            par = t % 2
            stB = gbuf[:, 6144 + par * 1024: 6144 + par * 1024 + 512]
            if stage_free[par] > 0:
                b.wait("vector", "KQ0" if par == 0 else "KQ1", stage_free[par])
            emit_rope(reg, stB, t, 4)
            qv = vfence()
            b.wait("sync", "SV", qv)
            stage_free[par] = b.dmat(
                AP(qoT, (4 * qq) * 1024 + t * 128, [[16384, 128], [1024, 4], [1, 128]]), stB,
                inc="KQ0" if par == 0 else "KQ1")
        return f

    def kv_scan(with_output):
        for c64 in range(16):
            src = (KVa, KVb)[c64 % 2]
            dst = (KVb, KVa)[c64 % 2]
            t, hh = c64 // 2, c64 % 2
            po = hh * 64
            for g in range(4):
                par = (c64 * 4 + g) % 2
                pp = PS[par]
                b.wait("vector", "ST")
                for hl in range(4):
                    h = 4 * g + hl
                    V(lambda e, t=t, po=po, g=g, hl=hl, h=h: e.tensor_scalar(
                        ktil[po:po + 64, hl * 128:(hl + 1) * 128],
                        ktok[po:po + 64, t * 512 + g * 128:t * 512 + (g + 1) * 128],
                        lamF[po:po + 64, 32 + h:33 + h], None, AOT.mult))
                if with_output:
                    for hl in range(4):
                        h = 4 * g + hl
                        V(lambda e, t=t, po=po, h=h, hl=hl: e.tensor_mul(
                            qtil[:, hl * 64:(hl + 1) * 64],
                            AP(qoT, h * 1024 + t * 128 + po, [[16384, 128], [1, 64]]),
                            ring[:, 1024 + h * 64:1024 + (h + 1) * 64]))
                vprep = V(lambda e: e.tensor_copy(dumV[:, 1:2], dumV[:, 5:6]), inc="SV")
                b.wait("tensor", "SV", vprep)
                if with_output:
                    smm = Tt(lambda e, t=t, po=po, g=g, pp=pp: e.matmul(
                        pp[po:po + 64, 1024:1280],
                        kT[:, g * 1024 + t * 128 + po: g * 1024 + t * 128 + po + 64],
                        AP(qoT, (4 * g) * 1024 + t * 128 + po, [[16384, 128], [1024, 4], [1, 64]]),
                        start=True, stop=True), inc="ST")
                    b.wait("vector", "ST", smm)
                    scm_sl = scm[po:po + 64, par * 256:par * 256 + 256]
                    vm = V(lambda e, g=g, po=po, pp=pp, scm_sl=scm_sl: e.tensor_mul(
                        scm_sl, pp[po:po + 64, 1024:1280],
                        ring[po:po + 64, (4 * g) * 64:(4 * g + 4) * 64]), inc="SV")
                    b.wait("tensor", "SV", vm)
                    b.wait("tensor", "SS")
                    for hl in range(4):
                        h = 4 * g + hl
                        Tt(lambda e, t=t, po=po, g=g, hl=hl, pp=pp, par=par: e.matmul(
                            pp[:, 1536 + hl * 64:1536 + (hl + 1) * 64],
                            vtok[po:po + 64, t * 512 + g * 128:t * 512 + (g + 1) * 128],
                            scm[po:po + 64, par * 256 + hl * 64:par * 256 + (hl + 1) * 64],
                            start=True, stop=False))
                        Tt(lambda e, h=h, hl=hl, pp=pp: e.matmul(
                            pp[:, 1536 + hl * 64:1536 + (hl + 1) * 64],
                            KVbf[:, h * 128:(h + 1) * 128],
                            qtil[:, hl * 64:(hl + 1) * 64],
                            start=False, stop=True))
                for hl in range(4):
                    Tt(lambda e, t=t, po=po, g=g, hl=hl, pp=pp: e.matmul(
                        pp[:, 512 + hl * 128:512 + (hl + 1) * 128],
                        ktil[po:po + 64, hl * 128:(hl + 1) * 128],
                        vtok[po:po + 64, t * 512 + g * 128:t * 512 + (g + 1) * 128],
                        start=True, stop=True),
                       inc="ST" if hl == 3 else None)
                b.wait("vector", "ST")
                if with_output:
                    V(lambda e, t=t, po=po, g=g, pp=pp: e.tensor_copy(
                        AP(qoT, (4 * g) * 1024 + t * 128 + po, [[16384, 128], [1024, 4], [1, 64]]),
                        pp[:, 1536:1792]))
                for hl in range(4):
                    h = 4 * g + hl
                    V(lambda e, h=h, hl=hl, src=src, dst=dst, pp=pp: e.scalar_tensor_tensor(
                        dst[:, h * 128:(h + 1) * 128],
                        src[:, h * 128:(h + 1) * 128],
                        lamF[:, 48 + h:49 + h],
                        pp[:, 512 + hl * 128:512 + (hl + 1) * 128],
                        AOT.mult, AOT.add))
                vu = V(lambda e: e.tensor_copy(dumV[:, 2:3], dumV[:, 6:7]), inc="SV")
                if with_output and c64 < 15:
                    b.wait("scalar", "SV", vu)
                    for hl in range(4):
                        h = 4 * g + hl
                        Sc(lambda e, h=h, dst=dst: e.copy(
                            KVbf[:, h * 128:(h + 1) * 128], dst[:, h * 128:(h + 1) * 128]),
                            inc="SS" if hl == 3 else None)

    _dbg("scos", scos[:, :], (128, 1024))
    _dbg("ssin", ssin[:, :], (128, 1024))
    _dbg("diag", ring[:, 0:2048], (128, 2048))
    _dbg("qdec", ring[:, 2048:4096], (128, 2048))
    _dbg("lam", lamF[:, 0:64], (128, 64))
    ln_phase(x_pre)
    _bar(2)
    _dbg("lnTpre", lnT[:, :], (128, 16384))
    kouter_pass(ln_lhs, lambda k: wkT[k * 128:(k + 1) * 128, 0:512], evac_k(False))
    kouter_pass(ln_lhs, lambda k: wvT[k * 128:(k + 1) * 128, 0:512], evac_v)
    b.wait("tensor", "SV")
    b.wait("vector", "SW")
    _dbg("ktokpre", ktok[:, :], (128, 4096))
    _dbg("vtokpre", vtok[:, :], (128, 4096))
    kv_scan(with_output=False)
    _bar(3)
    _dbg("KV0", KVa[:, :], (128, 2048))

    b.wait("gpsimd", "SV")
    tb2 = b.dma("gpsimd", scos[:, :], tbl[:, 0:1024], "TB")
    tb2 = b.dma("gpsimd", ssin[:, :], tbl[:, 1024:2048], "TB")
    b.wait("vector", "TB", tb2)
    _dbg("scos0", scos[:, :], (128, 1024))
    ln_phase(x_own)
    _bar(4)
    _dbg("lnT1", lnT[:, :], (128, 16384))
    kouter_pass(ln_lhs, lambda k: wkT[k * 128:(k + 1) * 128, 0:512], evac_k(True))
    kouter_pass(ln_lhs, lambda k: wvT[k * 128:(k + 1) * 128, 0:512], evac_v)
    for qq in range(4):
        kouter_pass(ln_lhs,
                    lambda k, qq=qq: wqT[k * 128:(k + 1) * 128, qq * 512:(qq + 1) * 512],
                    evac_q(qq))
    _bar(5)
    _dbg("qT", qoT[:, :], (128, 16384))
    _dbg("kTd", kT[:, :], (128, 4096))
    _dbg("ktok1", ktok[:, :], (128, 4096))
    _dbg("vtok1", vtok[:, :], (128, 4096))

    b.wait("tensor", "KQ0")
    b.wait("tensor", "KQ1")
    b.wait("vector", "KQ0")
    b.wait("vector", "KQ1")
    b.wait("scalar", "SV")
    for h in range(16):
        Sc(lambda e, h=h: e.copy(KVbf[:, h * 128:(h + 1) * 128], KVa[:, h * 128:(h + 1) * 128]),
            inc="SS" if h == 15 else None)
    kv_scan(with_output=True)
    _bar(6)
    _dbg("oT", qoT[:, :], (128, 16384))

    def ao_base(t):
        if t < 3:
            return (aoF1, t * 2048, 6144)
        if t < 5:
            return (fring, (t - 3) * 2048, 4096)
        if t < 7:
            return (gbufF, (t - 5) * 2048, 4096)
        return (tgF, 2048, 4096)

    def ao_tile(t):
        tt, off, pitch = ao_base(t)
        return AP(tt, off, [[pitch, 128], [1, 2048]])

    def ao_slice(t, dq):
        tt, off, pitch = ao_base(t)
        return AP(tt, off + dq * 512, [[pitch, 128], [1, 512]])

    def evac_ao(dq):
        def f(t, reg, st):
            b.wait("vector", "ST", st)
            V(lambda e, reg=reg, t=t, dq=dq: e.tensor_copy(ao_slice(t, dq), reg),
              inc="SV" if t == NT - 1 else None)
        return f

    for dq in range(4):
        kouter_pass(
            lambda k, t: AP(qoT, k * 1024 + t * 128, [[16384, 128], [1, 128]]),
            lambda k, dq=dq: woT[k * 128:(k + 1) * 128, dq * 512:(dq + 1) * 512],
            evac_ao(dq), pre_waits=("KQ0", "KQ1"))

    _bar(7)
    _dbg("ao0", ao_tile(0), (128, 2048))
    _dbg("ao7", ao_tile(7), (128, 2048))
    b.wait("sync", "SV")
    prev_x1v = 0
    prev_store = 0
    for t in range(NT):
        at = ao_tile(t)
        sv = ln_stats(at, t % 8, sq_wait_sv=b.cnt["SV"])
        if prev_x1v > 0:
            b.wait("sync", "SV", prev_x1v)
        xl = b.dma("sync", xrl[:, :], x_own[t * 128:(t + 1) * 128, :], "XR")
        b.wait("vector", "SV", sv)
        V(lambda e, at=at, jc=t % 8: e.tensor_scalar(
            ringF[:, :], at, stat[:, 16 + jc:17 + jc], stat[:, 24 + jc:25 + jc],
            AOT.subtract, AOT.mult))
        b.wait("vector", "XR", xl)
        vdummy(2)
        V(lambda e, at=at: e.tensor_add(at, xrl[:, :], ringF[:, :]))
        x1v = vfence()
        prev_x1v = x1v
        b.wait("sync", "SV", x1v)
        b.dma("sync", x1_hbm[t * 128:(t + 1) * 128, :], at, "SP")
        vdummy(2)
        sv2 = ln_stats(at, t % 8, sq_wait_sv=x1v)
        if prev_store > 0:
            b.wait("scalar", "L2T", prev_store)
        a2 = ln_apply(at, stage2[:, :], t % 8, sv2)
        b.wait("sync", "SS", a2)
        prev_store = b.dmat(AP(lnT, t * 128, [[16384, 128], [1024, 16], [1, 128]]), stage2[:, :],
                            inc="L2T")

    _bar(8)
    _dbg("lnT2", lnT[:, :], (128, 16384))
    b.wait("tensor", "L2T")
    b.wait("vector", "SP")
    dstage_free = [0, 0]
    acc_sw = [[0] * 4 for _ in range(NT)]

    for sub in range(4):
        last_gm = 0
        for ht16 in range(16):
            ht = sub * 16 + ht16
            par = ht16 % 2
            pp = PS[par]
            bg, bu = par * 2, par * 2 + 1
            wmax = max(slot_free_st[bg * 4: bg * 4 + 4] + slot_free_st[bu * 4: bu * 4 + 4])
            if wmax > 0:
                b.wait("gpsimd", "ST", wmax)
            ldg = b.dma("gpsimd", warena[:, bg * 2048:(bg + 1) * 2048], wgP[ht, :, :], f"W{bg * 4}")
            ldu = b.dma("gpsimd", warena[:, bu * 2048:(bu + 1) * 2048], wuP[ht, :, :], f"W{bu * 4}")
            b.wait("tensor", "SS")
            b.wait("tensor", "SV")
            b.wait("tensor", f"W{bg * 4}", ldg)
            b.wait("tensor", f"W{bu * 4}", ldu)
            st = None
            for k in range(NKC):
                for half in range(2):
                    lhs_g = warena[:, bg * 2048 + k * 128: bg * 2048 + (k + 1) * 128]
                    lhs_u = warena[:, bu * 2048 + k * 128: bu * 2048 + (k + 1) * 128]
                    rhs = lnT[:, k * 1024 + half * 512: k * 1024 + half * 512 + 512]
                    Tt(lambda e, pp=pp, half=half, lhs_g=lhs_g, rhs=rhs, k=k: e.matmul(
                        pp[:, half * 512:half * 512 + 512], lhs_g, rhs,
                        start=(k == 0), stop=(k == NKC - 1)))
                    st = Tt(lambda e, pp=pp, half=half, lhs_u=lhs_u, rhs=rhs, k=k: e.matmul(
                        pp[:, 1024 + half * 512:1024 + half * 512 + 512], lhs_u, rhs,
                        start=(k == 0), stop=(k == NKC - 1)),
                        inc="ST" if (k == NKC - 1 and half == 1) else None)
            for i in list(range(bg * 4, bg * 4 + 4)) + list(range(bu * 4, bu * 4 + 4)):
                slot_free_st[i] = st
            b.wait("scalar", "ST", st)
            ss = Sc(lambda e, pp=pp: e.activation(scm[:, :], pp[:, 0:1024], AFT.Silu), inc="SS")
            b.wait("vector", "SS", ss)
            last_gm = V(lambda e, pp=pp, ht16=ht16: e.tensor_mul(
                gbuf[:, ht16 * 1024:(ht16 + 1) * 1024], scm[:, :], pp[:, 1024:2048]), inc="SV")
        b.wait("tensor", "SV", last_gm)
        if sub == 0:
            _dbg("g0", gbuf[:, :], (128, 16384))
        for dq in range(4):
            slots, loads = [], []
            for k in range(NKC):
                i = wslot()
                if slot_free_st[i] > 0:
                    b.wait("gpsimd", "ST", slot_free_st[i])
                sl = warena[:, i * 512:(i + 1) * 512]
                slots.append((i, sl))
                loads.append(b.dma("gpsimd", sl,
                                   wdT[sub * 2048 + k * 128: sub * 2048 + (k + 1) * 128,
                                       dq * 512:(dq + 1) * 512], f"W{i}"))
            b.wait("tensor", "SV")
            for k in range(NKC):
                b.wait("tensor", f"W{slots[k][0]}", loads[k])
                for t in range(NT):
                    reg = PS[t // 4][:, (t % 4) * 512:(t % 4) * 512 + 512]
                    lhs = AP(gbuf, k * 1024 + t * 128, [[16384, 128], [1, 128]])
                    if t == NT - 1:
                        st = Tt(lambda e, reg=reg, lhs=lhs, sl=slots[k][1], k=k: e.matmul(
                            reg, lhs, sl, start=(k == 0), stop=(k == NKC - 1)), inc="ST")
                        slot_free_st[slots[k][0]] = st
                    else:
                        Tt(lambda e, reg=reg, lhs=lhs, sl=slots[k][1], k=k: e.matmul(
                            reg, lhs, sl, start=(k == 0), stop=(k == NKC - 1)))
            stlast = b.cnt["ST"]
            b.wait("vector", "ST", stlast)
            for t in range(NT):
                reg = PS[t // 4][:, (t % 4) * 512:(t % 4) * 512 + 512]
                dsl = acc[:, t * 2048 + dq * 512: t * 2048 + dq * 512 + 512]
                if sub == 0:
                    V(lambda e, reg=reg, dsl=dsl: e.tensor_copy(dsl, reg),
                      inc="SV" if t == NT - 1 else None)
                else:
                    spar = t % 2
                    if dstage_free[spar] > 0:
                        b.wait("vector", "AC", b.cnt["AC"])
                    stg = ringF[:, spar * 512:spar * 512 + 512]
                    V(lambda e, reg=reg, stg=stg: e.tensor_copy(stg, reg))
                    vv = vfence()
                    b.wait("gpsimd", "SV", vv)
                    if acc_sw[t][dq] > 0:
                        b.wait("gpsimd", "AC", b.cnt["AC"])
                    sw = b.dma("gpsimd", dsl, stg, "AC", accum=True)
                    dstage_free[spar] = sw
                    acc_sw[t][dq] = sw

    _bar(9)
    _dbg("accd", acc[:, :], (128, 16384))
    b.wait("vector", "AC")
    b.wait("scalar", "AC")
    b.wait("scalar", "SV")
    b.wait("sync", "SP")
    prev_ov = 0
    slot_store = [0, 0]
    for t in range(NT):
        at = acc[:, t * 2048:(t + 1) * 2048]
        sv = ln_stats(at, t % 8, sq_wait_sv=b.cnt["SV"])
        if prev_ov > 0:
            b.wait("sync", "SV", prev_ov)
        xl = b.dma("sync", xrl[:, :], x1_hbm[t * 128:(t + 1) * 128, :], "XR")
        b.wait("vector", "SV", sv)
        V(lambda e, at=at, jc=t % 8: e.tensor_scalar(
            ringF[:, :], at, stat[:, 16 + jc:17 + jc], stat[:, 24 + jc:25 + jc],
            AOT.subtract, AOT.mult))
        b.wait("vector", "XR", xl)
        par = t % 2
        if slot_store[par] > 0:
            b.wait("vector", "SO0" if par == 0 else "SO1", slot_store[par])
        vdummy(2)
        V(lambda e, par=par: e.tensor_add(
            gbufF[:, par * 2048:(par + 1) * 2048], xrl[:, :], ringF[:, :]))
        ov = vfence()
        prev_ov = ov
        b.wait("sync", "SV", ov)
        slot_store[par] = b.dma("sync", out_hbm[t * 128:(t + 1) * 128, :],
                                gbufF[:, par * 2048:(par + 1) * 2048],
                                "SO0" if par == 0 else "SO1")
    for sname in ("XI", "XA", "XB", "XR", "LT0", "LT1", "L2T", "KQ0", "KQ1",
                  "SP", "SO0", "SO1"):
        b.wait("sync", sname)
    for sname in ["AC"] + [f"W{i}" for i in range(16)]:
        b.wait("gpsimd", sname)

    nc.detect_race_conditions = False
    with nc.Block() as block:
        @block.sync
        def _(e):
            for fn in b.q["sync"]:
                fn(e)

        @block.gpsimd
        def _(e):
            for fn in b.q["gpsimd"]:
                fn(e)

        @block.tensor
        def _(e):
            for fn in b.q["tensor"]:
                fn(e)

        @block.vector
        def _(e):
            for fn in b.q["vector"]:
                fn(e)

        @block.scalar
        def _(e):
            for fn in b.q["scalar"]:
                fn(e)

    return nc


_NC_CACHE = {}


def _get_nc():
    if "nc" not in _NC_CACHE:
        _NC_CACHE["nc"] = build_program()
    return _NC_CACHE["nc"]


_TBL_CACHE = {}


def _host_tbl(half):
    if half in _TBL_CACHE:
        return _TBL_CACHE[half]
    import jax
    import jax.numpy as jnp
    cpu = jax.devices("cpu")[0]
    with jax.default_device(cpu):
        inv_freq = 1.0 / 10000.0 ** (jnp.arange(0, 128, 2, dtype=jnp.float32) / 128)
        pos = jnp.arange(2048, dtype=jnp.float32)
        freqs = pos[:, None] * inv_freq[None, :]
        cos = np.asarray(jnp.cos(freqs))
        sin = np.asarray(jnp.sin(freqs))
        lam = jnp.exp2(-8.0 * jnp.arange(1, 17, dtype=jnp.float32) / 16.0)
        r = jnp.arange(64, dtype=jnp.float32)
        diff = r[:, None] - r[None, :]
        diag = np.asarray(jnp.where(diff >= 0, jnp.exp(-lam[:, None, None] * diff), 0.0))
        qdec = np.asarray(jnp.exp(-lam[:, None] * (r + 1.0)))
        kdec = np.asarray(jnp.exp(-lam[:, None] * (64.0 - 1.0 - r)))
        blk = np.asarray(jnp.exp(-lam * 64.0))
    tblv = np.zeros((128, 6176), np.float32)
    p = np.arange(128)
    for t in range(8):
        own = half * 1024 + t * 128 + p
        pre = t * 128 + p
        tblv[:, t * 128:t * 128 + 64] = cos[own]
        tblv[:, t * 128 + 64:t * 128 + 128] = cos[own]
        tblv[:, 1024 + t * 128:1024 + t * 128 + 64] = sin[own]
        tblv[:, 1024 + t * 128 + 64:1024 + t * 128 + 128] = sin[own]
        tblv[:, 2048 + t * 128:2048 + t * 128 + 64] = cos[pre]
        tblv[:, 2048 + t * 128 + 64:2048 + t * 128 + 128] = cos[pre]
        tblv[:, 3072 + t * 128:3072 + t * 128 + 64] = sin[pre]
        tblv[:, 3072 + t * 128 + 64:3072 + t * 128 + 128] = sin[pre]
    for h in range(16):
        tblv[0:64, 4096 + h * 64:4096 + (h + 1) * 64] = diag[h].T
        tblv[64:128, 4096 + h * 64:4096 + (h + 1) * 64] = diag[h].T
        tblv[:, 5120 + h * 64:5120 + (h + 1) * 64] = qdec[h][None, :]
        tblv[:, 6144 + h] = kdec[h][p % 64]
        tblv[:, 6160 + h] = blk[h]
    tblv = np.ascontiguousarray(tblv)
    _TBL_CACHE[half] = tblv
    return tblv


def kernel(L0_attn_x, L0_attn_wq, L0_attn_wk, L0_attn_wv, L0_attn_wo,
           L0_ffn_w_gate, L0_ffn_w_up, L0_ffn_w_down):
    x = np.ascontiguousarray(np.asarray(L0_attn_x, dtype=np.float32))
    wq = np.asarray(L0_attn_wq, dtype=np.float32)
    wk = np.asarray(L0_attn_wk, dtype=np.float32)
    wv = np.asarray(L0_attn_wv, dtype=np.float32)
    wo = np.asarray(L0_attn_wo, dtype=np.float32)
    wg = np.asarray(L0_ffn_w_gate, dtype=np.float32)
    wu = np.asarray(L0_ffn_w_up, dtype=np.float32)
    wd = np.asarray(L0_ffn_w_down, dtype=np.float32)

    wqT = np.ascontiguousarray(wq.T)
    wkT = np.ascontiguousarray(wk.T)
    wvT = np.ascontiguousarray(wv.T)
    woT = np.ascontiguousarray(wo.T)
    wgP = np.ascontiguousarray(
        wg.reshape(64, 128, 16, 128).transpose(0, 3, 2, 1).reshape(64, 128, 2048))
    wuP = np.ascontiguousarray(
        wu.reshape(64, 128, 16, 128).transpose(0, 3, 2, 1).reshape(64, 128, 2048))
    wdT = np.ascontiguousarray(wd.T)

    zeros = np.zeros((T, D), np.float32)
    in_maps = []
    for core in range(8):
        bb, half = core // 2, core % 2
        xs = np.ascontiguousarray(x[bb, half * T:(half + 1) * T, :])
        xp = np.ascontiguousarray(x[bb, 0:T, :]) if half == 1 else zeros
        in_maps.append(dict(
            x_own=xs, x_pre=xp, wqT=wqT, wkT=wkT, wvT=wvT, woT=woT,
            wgP=wgP, wuP=wuP, wdT=wdT, tbl=_host_tbl(half)))

    nc = _get_nc()
    res = run_bass_kernel_spmd(nc, in_maps, core_ids=list(range(8)))
    out = np.empty((B, S, D), np.float32)
    for core in range(8):
        bb, half = core // 2, core % 2
        out[bb, half * T:(half + 1) * T, :] = res.results[core]["out"]
    return out


# version 09
# speedup vs baseline: 1.1009x; 1.1009x over previous
import numpy as np
import concourse.bass as bass
import concourse.mybir as mybir
from concourse.bass_utils import run_bass_kernel_spmd

F32 = mybir.dt.float32
BF16 = mybir.dt.bfloat16
AOT = mybir.AluOpType
AFT = mybir.ActivationFunctionType
AXX = mybir.AxisListType.X

B, S, D = 4, 2048, 2048
HID = 8192
T, NT, NKC, NCH = 1024, 8, 16, 8
EPS = 1e-5

ENGS = ("sync", "gpsimd", "tensor", "vector", "scalar")
DEBUG = False
DEBUG_BARRIER = True
BARRIERS = set()


class Bld:
    def __init__(self, nc):
        self.nc = nc
        self.q = {e: [] for e in ENGS}
        self.sems = {}
        self.cnt = {}
        names = ["SX", "SW", "ST", "SV", "SS", "SG", "TB", "XI", "XA", "XB", "XR",
                 "LT0", "LT1", "L2T", "KQ0", "KQ1", "SP", "SO0", "SO1", "AC"]
        names += [f"W{i}" for i in range(16)]
        names += [f"AC{t}_{dq}" for t in range(8) for dq in range(4)]
        for s in names:
            self.sems[s] = nc.alloc_semaphore(s)
            self.cnt[s] = 0

    def emit(self, eng, fn):
        self.q[eng].append(fn)

    def wait(self, eng, sem, val=None):
        v = self.cnt[sem] if val is None else val
        if v <= 0:
            return
        s = self.sems[sem]
        self.emit(eng, lambda e, s=s, v=v: e.wait_ge(s, v))

    def op(self, eng, fn, inc=None, amt=1):
        if inc is None:
            self.emit(eng, fn)
            return None
        s = self.sems[inc]
        self.cnt[inc] += amt
        self.emit(eng, lambda e, s=s, a=amt, f=fn: f(e).then_inc(s, a))
        return self.cnt[inc]

    def dma(self, eng, out_ap, in_ap, inc="SW", accum=False):
        s = self.sems[inc]
        self.cnt[inc] += 16
        if accum:
            self.emit(eng, lambda e, o=out_ap, i=in_ap, s=s:
                      e.dma_start(o, i, accum_op=AOT.add).then_inc(s, 16))
        else:
            self.emit(eng, lambda e, o=out_ap, i=in_ap, s=s:
                      e.dma_start(o, i).then_inc(s, 16))
        return self.cnt[inc]

    def dmat(self, out_ap, in_ap, inc="SX"):
        s = self.sems[inc]
        self.cnt[inc] += 16
        self.emit("sync", lambda e, o=out_ap, i=in_ap, s=s:
                  e.dma_start_transpose(o, i).then_inc(s, 16))
        return self.cnt[inc]


def AP(t, off, dims):
    h = t.tensor if hasattr(t, "tensor") else t
    return bass.AP(h, off, dims)


def build_program():
    nc = bass.Bass("TRN2", target_bir_lowering=False)

    x_own = nc.dram_tensor("x_own", [T, D], F32, kind="ExternalInput").ap()
    x_pre = nc.dram_tensor("x_pre", [T, D], F32, kind="ExternalInput").ap()
    wqT = nc.dram_tensor("wqT", [D, 2048], F32, kind="ExternalInput").ap()
    wkT = nc.dram_tensor("wkT", [D, 512], F32, kind="ExternalInput").ap()
    wvT = nc.dram_tensor("wvT", [D, 512], F32, kind="ExternalInput").ap()
    woT = nc.dram_tensor("woT", [2048, D], F32, kind="ExternalInput").ap()
    wgP = nc.dram_tensor("wgP", [64, 128, 2048], F32, kind="ExternalInput").ap()
    wuP = nc.dram_tensor("wuP", [64, 128, 2048], F32, kind="ExternalInput").ap()
    wdT = nc.dram_tensor("wdT", [HID, D], F32, kind="ExternalInput").ap()
    tbl = nc.dram_tensor("tbl", [128, 6176], F32, kind="ExternalInput").ap()
    out_hbm = nc.dram_tensor("out", [T, D], F32, kind="ExternalOutput").ap()
    x1_hbm = nc.dram_tensor("x1spill", [T, D], F32, kind="Internal").ap()

    SB = 16640
    _A = nc.alloc_sbuf_tensor_at
    A = lambda name, shape, dt, offset: _A(name, shape, dt, offset=SB + offset)
    fring = A("fring", [128, 4096], F32, offset=0)
    lnT = A("lnT", [128, 16384], BF16, offset=16384)
    gbuf = A("gbuf", [128, 16384], BF16, offset=49152)
    gbufF = A("gbufF", [128, 4096], F32, offset=49152)
    tgF = A("tgF", [128, 4096], F32, offset=65536)
    qoT = A("qoT", [128, 16384], BF16, offset=81920)
    kT = A("kT", [128, 4096], BF16, offset=114688)
    ktok = A("ktok", [128, 4096], BF16, offset=122880)
    vtok = A("vtok", [128, 4096], BF16, offset=131072)
    KVa = A("KVa", [128, 2048], F32, offset=139264)
    KVb = A("KVb", [128, 2048], F32, offset=147456)
    acc = A("acc", [128, 16384], F32, offset=81920)
    aoF1 = A("aoF1", [128, 6144], F32, offset=114688)
    stage2 = A("stage2", [128, 2048], BF16, offset=139264)
    xrl = A("xrl", [128, 2048], F32, offset=147456)
    KVbf = A("KVbf", [128, 2048], BF16, offset=155648)
    warena = A("warena", [128, 8192], BF16, offset=159744)
    ring = A("ring", [128, 4096], BF16, offset=176128)
    ringF = A("ringF", [128, 2048], F32, offset=176128)
    scos = A("scos", [128, 1024], BF16, offset=184320)
    ssin = A("ssin", [128, 1024], BF16, offset=186368)
    sm0 = 188416
    stat = A("stat", [128, 64], F32, offset=sm0)
    lamF = A("lamF", [128, 80], F32, offset=sm0 + 256)
    idxS = A("idxS", [128, 256], F32, offset=sm0 + 576)
    ktil = A("ktil", [128, 512], BF16, offset=sm0 + 1600)
    qtil = A("qtil", [128, 512], BF16, offset=sm0 + 2624)
    scm = A("scm", [128, 1024], BF16, offset=sm0 + 3648)
    dumV = A("dumV", [128, 16], F32, offset=sm0 + 5696)
    sqd = A("sqd", [128, 2048], BF16, offset=sm0 + 5888)
    scm2 = A("scm2", [128, 1024], BF16, offset=sm0 + 9984)
    dumB = A("dumB", [128, 128], F32, offset=sm0 + 12032)

    p0 = nc.alloc_psum_tensor("p0", [128, 2048], F32)
    p1 = nc.alloc_psum_tensor("p1", [128, 2048], F32)
    PS = (p0, p1)

    b = Bld(nc)

    def V(fn, inc=None):
        return b.op("vector", fn, inc=inc)

    def Sc(fn, inc=None):
        return b.op("scalar", fn, inc=inc)

    def Tt(fn, inc=None):
        return b.op("tensor", fn, inc=inc)

    def vdummy(n=2):
        for _ in range(n):
            V(lambda e: e.tensor_copy(dumB[:, 0:64], dumB[:, 64:128]))

    def vfence(inc="SV"):
        vdummy(2)
        return V(lambda e: e.tensor_copy(dumB[:, 0:64], dumB[:, 64:128]), inc=inc)

    def sfence(inc="SS"):
        Sc(lambda e: e.copy(dumV[:, 0:1], dumV[:, 4:5]))
        return Sc(lambda e: e.copy(dumV[:, 1:2], dumV[:, 5:6]), inc=inc)

    dbg_outs = []

    def _dbg(name, src_ap, shape):
        if not DEBUG:
            return
        d = nc.dram_tensor("dbg_" + name, list(shape), F32, kind="ExternalOutput").ap()
        for sname in list(b.sems):
            if sname != "SW":
                b.wait("gpsimd", sname)
        b.dma("gpsimd", d[:, :], src_ap, "SW")
        if DEBUG_BARRIER:
            for eng in ENGS:
                b.wait(eng, "SW")
        dbg_outs.append(name)

    def _bar(pid):
        if pid not in BARRIERS:
            return
        for sname in list(b.sems):
            b.wait("gpsimd", sname)
        b.op("gpsimd", lambda e: e.memset(dumV[:, 8:9], 0.0), inc="SG", amt=1)
        for eng in ENGS:
            b.wait(eng, "SW")
            b.wait(eng, "SG")

    SQD = 8192

    def tg(c0, w=1):
        return tgF[:, c0:c0 + w]

    b.op("gpsimd", lambda e: e.memset(KVa[:, :], 0.0), inc="SG", amt=1)
    b.op("gpsimd", lambda e: e.memset(stat[:, :], 0.0), inc="SG", amt=1)
    b.op("gpsimd", lambda e: e.memset(dumV[:, :], 0.0), inc="SG", amt=1)
    b.op("gpsimd", lambda e: e.memset(dumB[:, :], 0.0), inc="SG", amt=1)
    for _eng in ("vector", "scalar", "tensor", "sync"):
        b.wait(_eng, "SG")
    b.dma("gpsimd", ring[:, 0:1024], tbl[:, 4096:5120], "TB")
    b.dma("gpsimd", ring[:, 1024:2048], tbl[:, 5120:6144], "TB")
    b.dma("gpsimd", lamF[:, 32:48], tbl[:, 6144:6160], "TB")
    b.dma("gpsimd", lamF[:, 48:64], tbl[:, 6160:6176], "TB")
    tb1 = b.dma("gpsimd", scos[:, :], tbl[:, 2048:3072], "TB")
    tb1 = b.dma("gpsimd", ssin[:, :], tbl[:, 3072:4096], "TB")
    b.wait("vector", "TB", tb1)

    def ln_stats(src_tile, jc, sq_wait_sv=None):
        V(lambda e, s=src_tile, jc=jc: e.reduce_sum(stat[:, jc:jc + 1], s, AXX))
        if sq_wait_sv is not None:
            b.wait("scalar", "SV", sq_wait_sv)
        ss = Sc(lambda e, s=src_tile, jc=jc: e.activation(
            sqd[:, :], s, AFT.Square, accum_out=stat[:, 8 + jc:9 + jc]),
            inc="SS")
        b.wait("vector", "SS", ss)
        vdummy(2)
        V(lambda e, jc=jc: e.tensor_scalar_mul(stat[:, 16 + jc:17 + jc], stat[:, jc:jc + 1], 1.0 / 2048.0))
        vdummy(2)
        V(lambda e, jc=jc: e.tensor_mul(stat[:, 40 + jc:41 + jc], stat[:, 16 + jc:17 + jc], stat[:, 16 + jc:17 + jc]))
        V(lambda e, jc=jc: e.tensor_scalar_mul(stat[:, 48 + jc:49 + jc], stat[:, 8 + jc:9 + jc], 1.0 / 2048.0))
        vdummy(2)
        V(lambda e, jc=jc: e.tensor_sub(stat[:, 56 + jc:57 + jc], stat[:, 48 + jc:49 + jc], stat[:, 40 + jc:41 + jc]))
        vdummy(2)
        vv = V(lambda e, jc=jc: e.tensor_scalar_add(stat[:, 40 + jc:41 + jc], stat[:, 56 + jc:57 + jc], EPS), inc="SV")
        b.wait("scalar", "SV", vv)
        ss = Sc(lambda e, jc=jc: e.activation(stat[:, 48 + jc:49 + jc], stat[:, 40 + jc:41 + jc], AFT.Sqrt), inc="SS")
        b.wait("vector", "SS", ss)
        V(lambda e, jc=jc: e.reciprocal(stat[:, 24 + jc:25 + jc], stat[:, 48 + jc:49 + jc]))
        vdummy(2)
        V(lambda e, jc=jc: e.tensor_mul(stat[:, 56 + jc:57 + jc], stat[:, 16 + jc:17 + jc], stat[:, 24 + jc:25 + jc]))
        vdummy(2)
        return V(lambda e, jc=jc: e.tensor_scalar_mul(stat[:, 32 + jc:33 + jc], stat[:, 56 + jc:57 + jc], -1.0), inc="SV")

    def ln_apply(src_tile, dst_tile, jc, sv):
        b.wait("scalar", "SV", sv)
        Sc(lambda e, s=src_tile, d=dst_tile, jc=jc: e.activation(
            d, s, AFT.Identity, bias=stat[:, 32 + jc:33 + jc], scale=stat[:, 24 + jc:25 + jc]))
        return sfence()

    def ln_phase(x_src):
        applies = {}
        transposes = {}
        for j in range(NT):
            par = j % 2
            xsem = "XA" if par == 0 else "XB"
            ltsem = "LT0" if par == 0 else "LT1"
            slot = fring[:, par * 2048:par * 2048 + 2048]
            if j >= 2:
                b.wait("sync", "SS", applies[j - 2])
            ld = b.dma("sync", slot, x_src[j * 128:(j + 1) * 128, :], xsem)
            b.wait("vector", xsem, ld)
            b.wait("scalar", xsem, ld)
            sv = ln_stats(slot, j % 8)
            stg = gbuf[:, par * 2048:par * 2048 + 2048]
            if j >= 2:
                b.wait("scalar", ltsem, transposes[j - 2])
            a = ln_apply(slot, stg, j % 8, sv)
            applies[j] = a
            b.wait("sync", "SS", a)
            transposes[j] = b.dmat(AP(lnT, j * 128, [[16384, 128], [1024, 16], [1, 128]]), stg, inc=ltsem)

    wslot_state = {"next": 0}
    slot_free_st = [0] * 16

    def wslot():
        i = wslot_state["next"]
        wslot_state["next"] = (i + 1) % 16
        return i

    def kouter_pass(lhs_fn, w_src_fn, evac_fn, pre_waits=()):
        slots, loads = [], []
        for k in range(NKC):
            i = wslot()
            if slot_free_st[i] > 0:
                b.wait("gpsimd", "ST", slot_free_st[i])
            sl = warena[:, i * 512:(i + 1) * 512]
            slots.append((i, sl))
            loads.append(b.dma("gpsimd", sl, w_src_fn(k), f"W{i}"))
        b.wait("tensor", "SV")
        b.wait("tensor", "SS")
        for sem in pre_waits:
            b.wait("tensor", sem)
        for k in range(NKC):
            b.wait("tensor", f"W{slots[k][0]}", loads[k])
            for t in range(NT):
                reg = PS[t // 4][:, (t % 4) * 512:(t % 4) * 512 + 512]
                lhs = lhs_fn(k, t)
                if t == NT - 1:
                    st = Tt(lambda e, reg=reg, lhs=lhs, sl=slots[k][1], k=k: e.matmul(
                        reg, lhs, sl, start=(k == 0), stop=(k == NKC - 1)), inc="ST")
                    slot_free_st[slots[k][0]] = st
                else:
                    Tt(lambda e, reg=reg, lhs=lhs, sl=slots[k][1], k=k: e.matmul(
                        reg, lhs, sl, start=(k == 0), stop=(k == NKC - 1)))
        stlast = b.cnt["ST"]
        for t in range(NT):
            reg = PS[t // 4][:, (t % 4) * 512:(t % 4) * 512 + 512]
            evac_fn(t, reg, stlast)

    def ln_lhs(k, t):
        return AP(lnT, k * 1024 + t * 128, [[16384, 128], [1, 128]])

    def emit_rope(psrc, dst, t, nh):
        pT, po = psrc.tensor, psrc.offset
        dstT, do = dst.tensor, dst.offset
        cos_lo = AP(scos, t * 128, [[1024, 128], [0, nh], [1, 64]])
        sin_lo = AP(ssin, t * 128, [[1024, 128], [0, nh], [1, 64]])
        q_lo = bass.AP(pT, po, [[2048, 128], [128, nh], [1, 64]])
        q_hi = bass.AP(pT, po + 64, [[2048, 128], [128, nh], [1, 64]])
        o_lo = bass.AP(dstT, do, [[16384, 128], [128, nh], [1, 64]])
        o_hi = bass.AP(dstT, do + 64, [[16384, 128], [128, nh], [1, 64]])
        m1 = AP(tgF, 96, [[4096, 128], [64, nh], [1, 64]])
        m2 = AP(tgF, 608, [[4096, 128], [64, nh], [1, 64]])
        m3 = AP(tgF, 1120, [[4096, 128], [64, nh], [1, 64]])
        m4 = AP(tgF, 1632, [[4096, 128], [64, nh], [1, 64]])
        V(lambda e: e.tensor_mul(m1, q_lo, cos_lo))
        V(lambda e: e.tensor_mul(m2, q_hi, sin_lo))
        V(lambda e: e.tensor_mul(m3, q_hi, cos_lo))
        V(lambda e: e.tensor_mul(m4, q_lo, sin_lo))
        vdummy(2)
        V(lambda e: e.tensor_sub(o_lo, m1, m2))
        vdummy(2)
        V(lambda e: e.tensor_add(o_hi, m3, m4))
        vdummy(4)

    stage_free = {0: 0, 1: 0}

    def evac_k(to_kT):
        def f(t, reg, st):
            b.wait("vector", "ST", st)
            par = t % 2
            stB = gbuf[:, 6144 + par * 1024: 6144 + par * 1024 + 512]
            if stage_free[par] > 0:
                b.wait("vector", "KQ0" if par == 0 else "KQ1", stage_free[par])
            emit_rope(reg, stB, t, 4)
            V(lambda e, t=t, stB=stB: e.tensor_copy(
                AP(ktok, t * 512, [[4096, 128], [1, 512]]), stB))
            kv = vfence()
            if to_kT:
                b.wait("sync", "SV", kv)
                stage_free[par] = b.dmat(
                    AP(kT, t * 128, [[4096, 128], [1024, 4], [1, 128]]), stB,
                    inc="KQ0" if par == 0 else "KQ1")
        return f

    def evac_v(t, reg, st):
        b.wait("vector", "ST", st)
        V(lambda e, reg=reg, t=t: e.tensor_copy(
            AP(vtok, t * 512, [[4096, 128], [1, 512]]), reg),
          inc="SV" if t == NT - 1 else None)

    def evac_q(qq):
        def f(t, reg, st):
            b.wait("vector", "ST", st)
            par = t % 2
            stB = gbuf[:, 6144 + par * 1024: 6144 + par * 1024 + 512]
            if stage_free[par] > 0:
                b.wait("vector", "KQ0" if par == 0 else "KQ1", stage_free[par])
            emit_rope(reg, stB, t, 4)
            qv = vfence()
            b.wait("sync", "SV", qv)
            stage_free[par] = b.dmat(
                AP(qoT, (4 * qq) * 1024 + t * 128, [[16384, 128], [1024, 4], [1, 128]]), stB,
                inc="KQ0" if par == 0 else "KQ1")
        return f

    def kv_scan(with_output):
        for c64 in range(16):
            src = (KVa, KVb)[c64 % 2]
            dst = (KVb, KVa)[c64 % 2]
            t, hh = c64 // 2, c64 % 2
            po = hh * 64
            for g in range(4):
                par = (c64 * 4 + g) % 2
                pp = PS[par]
                b.wait("vector", "ST")
                for hl in range(4):
                    h = 4 * g + hl
                    V(lambda e, t=t, po=po, g=g, hl=hl, h=h: e.tensor_scalar(
                        ktil[po:po + 64, hl * 128:(hl + 1) * 128],
                        ktok[po:po + 64, t * 512 + g * 128:t * 512 + (g + 1) * 128],
                        lamF[po:po + 64, 32 + h:33 + h], None, AOT.mult))
                if with_output:
                    for hl in range(4):
                        h = 4 * g + hl
                        V(lambda e, t=t, po=po, h=h, hl=hl: e.tensor_mul(
                            qtil[:, hl * 64:(hl + 1) * 64],
                            AP(qoT, h * 1024 + t * 128 + po, [[16384, 128], [1, 64]]),
                            ring[:, 1024 + h * 64:1024 + (h + 1) * 64]))
                vprep = V(lambda e: e.tensor_copy(dumV[:, 1:2], dumV[:, 5:6]), inc="SV")
                b.wait("tensor", "SV", vprep)
                if with_output:
                    smm = Tt(lambda e, t=t, po=po, g=g, pp=pp: e.matmul(
                        pp[po:po + 64, 1024:1280],
                        kT[:, g * 1024 + t * 128 + po: g * 1024 + t * 128 + po + 64],
                        AP(qoT, (4 * g) * 1024 + t * 128 + po, [[16384, 128], [1024, 4], [1, 64]]),
                        start=True, stop=True), inc="ST")
                    b.wait("vector", "ST", smm)
                    scm_sl = scm[po:po + 64, par * 256:par * 256 + 256]
                    vm = V(lambda e, g=g, po=po, pp=pp, scm_sl=scm_sl: e.tensor_mul(
                        scm_sl, pp[po:po + 64, 1024:1280],
                        ring[po:po + 64, (4 * g) * 64:(4 * g + 4) * 64]), inc="SV")
                    b.wait("tensor", "SV", vm)
                    b.wait("tensor", "SS")
                    for hl in range(4):
                        h = 4 * g + hl
                        Tt(lambda e, t=t, po=po, g=g, hl=hl, pp=pp, par=par: e.matmul(
                            pp[:, 1536 + hl * 64:1536 + (hl + 1) * 64],
                            vtok[po:po + 64, t * 512 + g * 128:t * 512 + (g + 1) * 128],
                            scm[po:po + 64, par * 256 + hl * 64:par * 256 + (hl + 1) * 64],
                            start=True, stop=False))
                        Tt(lambda e, h=h, hl=hl, pp=pp: e.matmul(
                            pp[:, 1536 + hl * 64:1536 + (hl + 1) * 64],
                            KVbf[:, h * 128:(h + 1) * 128],
                            qtil[:, hl * 64:(hl + 1) * 64],
                            start=False, stop=True))
                for hl in range(4):
                    Tt(lambda e, t=t, po=po, g=g, hl=hl, pp=pp: e.matmul(
                        pp[:, 512 + hl * 128:512 + (hl + 1) * 128],
                        ktil[po:po + 64, hl * 128:(hl + 1) * 128],
                        vtok[po:po + 64, t * 512 + g * 128:t * 512 + (g + 1) * 128],
                        start=True, stop=True),
                       inc="ST" if hl == 3 else None)
                b.wait("vector", "ST")
                if with_output:
                    V(lambda e, t=t, po=po, g=g, pp=pp: e.tensor_copy(
                        AP(qoT, (4 * g) * 1024 + t * 128 + po, [[16384, 128], [1024, 4], [1, 64]]),
                        pp[:, 1536:1792]))
                for hl in range(4):
                    h = 4 * g + hl
                    V(lambda e, h=h, hl=hl, src=src, dst=dst, pp=pp: e.scalar_tensor_tensor(
                        dst[:, h * 128:(h + 1) * 128],
                        src[:, h * 128:(h + 1) * 128],
                        lamF[:, 48 + h:49 + h],
                        pp[:, 512 + hl * 128:512 + (hl + 1) * 128],
                        AOT.mult, AOT.add))
                vu = V(lambda e: e.tensor_copy(dumV[:, 2:3], dumV[:, 6:7]), inc="SV")
                if with_output and c64 < 15:
                    b.wait("scalar", "SV", vu)
                    for hl in range(4):
                        h = 4 * g + hl
                        Sc(lambda e, h=h, dst=dst: e.copy(
                            KVbf[:, h * 128:(h + 1) * 128], dst[:, h * 128:(h + 1) * 128]),
                            inc="SS" if hl == 3 else None)

    _dbg("scos", scos[:, :], (128, 1024))
    _dbg("ssin", ssin[:, :], (128, 1024))
    _dbg("diag", ring[:, 0:2048], (128, 2048))
    _dbg("qdec", ring[:, 2048:4096], (128, 2048))
    _dbg("lam", lamF[:, 0:64], (128, 64))
    ln_phase(x_pre)
    _bar(2)
    _dbg("lnTpre", lnT[:, :], (128, 16384))
    kouter_pass(ln_lhs, lambda k: wkT[k * 128:(k + 1) * 128, 0:512], evac_k(False))
    kouter_pass(ln_lhs, lambda k: wvT[k * 128:(k + 1) * 128, 0:512], evac_v)
    b.wait("tensor", "SV")
    b.wait("vector", "SW")
    _dbg("ktokpre", ktok[:, :], (128, 4096))
    _dbg("vtokpre", vtok[:, :], (128, 4096))
    kv_scan(with_output=False)
    _bar(3)
    _dbg("KV0", KVa[:, :], (128, 2048))

    b.wait("gpsimd", "SV")
    tb2 = b.dma("gpsimd", scos[:, :], tbl[:, 0:1024], "TB")
    tb2 = b.dma("gpsimd", ssin[:, :], tbl[:, 1024:2048], "TB")
    b.wait("vector", "TB", tb2)
    _dbg("scos0", scos[:, :], (128, 1024))
    ln_phase(x_own)
    _bar(4)
    _dbg("lnT1", lnT[:, :], (128, 16384))
    kouter_pass(ln_lhs, lambda k: wkT[k * 128:(k + 1) * 128, 0:512], evac_k(True))
    kouter_pass(ln_lhs, lambda k: wvT[k * 128:(k + 1) * 128, 0:512], evac_v)
    for qq in range(4):
        kouter_pass(ln_lhs,
                    lambda k, qq=qq: wqT[k * 128:(k + 1) * 128, qq * 512:(qq + 1) * 512],
                    evac_q(qq))
    _bar(5)
    _dbg("qT", qoT[:, :], (128, 16384))
    _dbg("kTd", kT[:, :], (128, 4096))
    _dbg("ktok1", ktok[:, :], (128, 4096))
    _dbg("vtok1", vtok[:, :], (128, 4096))

    b.wait("tensor", "KQ0")
    b.wait("tensor", "KQ1")
    b.wait("vector", "KQ0")
    b.wait("vector", "KQ1")
    b.wait("scalar", "SV")
    for h in range(16):
        Sc(lambda e, h=h: e.copy(KVbf[:, h * 128:(h + 1) * 128], KVa[:, h * 128:(h + 1) * 128]),
            inc="SS" if h == 15 else None)
    kv_scan(with_output=True)
    _bar(6)
    _dbg("oT", qoT[:, :], (128, 16384))

    def ao_base(t):
        if t < 3:
            return (aoF1, t * 2048, 6144)
        if t < 5:
            return (fring, (t - 3) * 2048, 4096)
        if t < 7:
            return (gbufF, (t - 5) * 2048, 4096)
        return (tgF, 2048, 4096)

    def ao_tile(t):
        tt, off, pitch = ao_base(t)
        return AP(tt, off, [[pitch, 128], [1, 2048]])

    def ao_slice(t, dq):
        tt, off, pitch = ao_base(t)
        return AP(tt, off + dq * 512, [[pitch, 128], [1, 512]])

    def evac_ao(dq):
        def f(t, reg, st):
            b.wait("vector", "ST", st)
            V(lambda e, reg=reg, t=t, dq=dq: e.tensor_copy(ao_slice(t, dq), reg),
              inc="SV" if t == NT - 1 else None)
        return f

    for dq in range(4):
        kouter_pass(
            lambda k, t: AP(qoT, k * 1024 + t * 128, [[16384, 128], [1, 128]]),
            lambda k, dq=dq: woT[k * 128:(k + 1) * 128, dq * 512:(dq + 1) * 512],
            evac_ao(dq), pre_waits=("KQ0", "KQ1"))

    _bar(7)
    _dbg("ao0", ao_tile(0), (128, 2048))
    _dbg("ao7", ao_tile(7), (128, 2048))
    b.wait("sync", "SV")
    prev_x1v = 0
    prev_store = 0
    for t in range(NT):
        at = ao_tile(t)
        sv = ln_stats(at, t % 8, sq_wait_sv=b.cnt["SV"])
        if prev_x1v > 0:
            b.wait("sync", "SV", prev_x1v)
        xl = b.dma("sync", xrl[:, :], x_own[t * 128:(t + 1) * 128, :], "XR")
        b.wait("vector", "SV", sv)
        V(lambda e, at=at, jc=t % 8: e.tensor_scalar(
            ringF[:, :], at, stat[:, 16 + jc:17 + jc], stat[:, 24 + jc:25 + jc],
            AOT.subtract, AOT.mult))
        b.wait("vector", "XR", xl)
        vdummy(2)
        V(lambda e, at=at: e.tensor_add(at, xrl[:, :], ringF[:, :]))
        x1v = vfence()
        prev_x1v = x1v
        b.wait("sync", "SV", x1v)
        b.dma("sync", x1_hbm[t * 128:(t + 1) * 128, :], at, "SP")
        vdummy(2)
        sv2 = ln_stats(at, t % 8, sq_wait_sv=x1v)
        if prev_store > 0:
            b.wait("scalar", "L2T", prev_store)
        a2 = ln_apply(at, stage2[:, :], t % 8, sv2)
        b.wait("sync", "SS", a2)
        prev_store = b.dmat(AP(lnT, t * 128, [[16384, 128], [1024, 16], [1, 128]]), stage2[:, :],
                            inc="L2T")

    _bar(8)
    _dbg("lnT2", lnT[:, :], (128, 16384))
    b.wait("tensor", "L2T")
    b.wait("vector", "SP")
    dstage_free = [None, None, None, None]
    acc_sw = [[0] * 4 for _ in range(NT)]

    for sub in range(4):
        last_gm = 0
        for ht16 in range(16):
            ht = sub * 16 + ht16
            par = ht16 % 2
            pp = PS[par]
            bg, bu = par * 2, par * 2 + 1
            wmax = max(slot_free_st[bg * 4: bg * 4 + 4] + slot_free_st[bu * 4: bu * 4 + 4])
            if wmax > 0:
                b.wait("gpsimd", "ST", wmax)
            ldg = b.dma("gpsimd", warena[:, bg * 2048:(bg + 1) * 2048], wgP[ht, :, :], f"W{bg * 4}")
            ldu = b.dma("gpsimd", warena[:, bu * 2048:(bu + 1) * 2048], wuP[ht, :, :], f"W{bu * 4}")
            b.wait("tensor", "SS")
            b.wait("tensor", "SV")
            b.wait("tensor", f"W{bg * 4}", ldg)
            b.wait("tensor", f"W{bu * 4}", ldu)
            st = None
            for k in range(NKC):
                for half in range(2):
                    lhs_g = warena[:, bg * 2048 + k * 128: bg * 2048 + (k + 1) * 128]
                    lhs_u = warena[:, bu * 2048 + k * 128: bu * 2048 + (k + 1) * 128]
                    rhs = lnT[:, k * 1024 + half * 512: k * 1024 + half * 512 + 512]
                    Tt(lambda e, pp=pp, half=half, lhs_g=lhs_g, rhs=rhs, k=k: e.matmul(
                        pp[:, half * 512:half * 512 + 512], lhs_g, rhs,
                        start=(k == 0), stop=(k == NKC - 1)))
                    st = Tt(lambda e, pp=pp, half=half, lhs_u=lhs_u, rhs=rhs, k=k: e.matmul(
                        pp[:, 1024 + half * 512:1024 + half * 512 + 512], lhs_u, rhs,
                        start=(k == 0), stop=(k == NKC - 1)),
                        inc="ST" if (k == NKC - 1 and half == 1) else None)
            for i in list(range(bg * 4, bg * 4 + 4)) + list(range(bu * 4, bu * 4 + 4)):
                slot_free_st[i] = st
            b.wait("scalar", "ST", st)
            ss = Sc(lambda e, pp=pp: e.activation(scm[:, :], pp[:, 0:1024], AFT.Silu), inc="SS")
            b.wait("vector", "SS", ss)
            last_gm = V(lambda e, pp=pp, ht16=ht16: e.tensor_mul(
                gbuf[:, ht16 * 1024:(ht16 + 1) * 1024], scm[:, :], pp[:, 1024:2048]), inc="SV")
        b.wait("tensor", "SV", last_gm)
        if sub == 0:
            _dbg("g0", gbuf[:, :], (128, 16384))
        for dq in range(4):
            slots, loads = [], []
            for k in range(NKC):
                i = wslot()
                if slot_free_st[i] > 0:
                    b.wait("gpsimd", "ST", slot_free_st[i])
                sl = warena[:, i * 512:(i + 1) * 512]
                slots.append((i, sl))
                loads.append(b.dma("gpsimd", sl,
                                   wdT[sub * 2048 + k * 128: sub * 2048 + (k + 1) * 128,
                                       dq * 512:(dq + 1) * 512], f"W{i}"))
            b.wait("tensor", "SV")
            for k in range(NKC):
                b.wait("tensor", f"W{slots[k][0]}", loads[k])
                for t in range(NT):
                    reg = PS[t // 4][:, (t % 4) * 512:(t % 4) * 512 + 512]
                    lhs = AP(gbuf, k * 1024 + t * 128, [[16384, 128], [1, 128]])
                    if t == NT - 1:
                        st = Tt(lambda e, reg=reg, lhs=lhs, sl=slots[k][1], k=k: e.matmul(
                            reg, lhs, sl, start=(k == 0), stop=(k == NKC - 1)), inc="ST")
                        slot_free_st[slots[k][0]] = st
                    else:
                        Tt(lambda e, reg=reg, lhs=lhs, sl=slots[k][1], k=k: e.matmul(
                            reg, lhs, sl, start=(k == 0), stop=(k == NKC - 1)))
            stlast = b.cnt["ST"]
            b.wait("vector", "ST", stlast)
            for t in range(NT):
                reg = PS[t // 4][:, (t % 4) * 512:(t % 4) * 512 + 512]
                dsl = acc[:, t * 2048 + dq * 512: t * 2048 + dq * 512 + 512]
                if sub == 0:
                    V(lambda e, reg=reg, dsl=dsl: e.tensor_copy(dsl, reg),
                      inc="SV" if t == NT - 1 else None)
                else:
                    spar = t % 4
                    if dstage_free[spar] is not None:
                        b.wait("vector", *dstage_free[spar])
                    stg = ringF[:, spar * 512:spar * 512 + 512]
                    V(lambda e, reg=reg, stg=stg: e.tensor_copy(stg, reg))
                    vv = vfence()
                    b.wait("gpsimd", "SV", vv)
                    acsem = f"AC{t}_{dq}"
                    if acc_sw[t][dq] > 0:
                        b.wait("gpsimd", acsem, acc_sw[t][dq])
                    sw = b.dma("gpsimd", dsl, stg, acsem, accum=True)
                    dstage_free[spar] = (acsem, sw)
                    acc_sw[t][dq] = sw

    _bar(9)
    _dbg("accd", acc[:, :], (128, 16384))
    for t in range(8):
        for dq in range(4):
            b.wait("vector", f"AC{t}_{dq}")
            b.wait("scalar", f"AC{t}_{dq}")
    b.wait("scalar", "SV")
    b.wait("sync", "SP")
    prev_ov = 0
    slot_store = [0, 0]
    for t in range(NT):
        at = acc[:, t * 2048:(t + 1) * 2048]
        sv = ln_stats(at, t % 8, sq_wait_sv=b.cnt["SV"])
        if prev_ov > 0:
            b.wait("sync", "SV", prev_ov)
        xl = b.dma("sync", xrl[:, :], x1_hbm[t * 128:(t + 1) * 128, :], "XR")
        b.wait("vector", "SV", sv)
        V(lambda e, at=at, jc=t % 8: e.tensor_scalar(
            ringF[:, :], at, stat[:, 16 + jc:17 + jc], stat[:, 24 + jc:25 + jc],
            AOT.subtract, AOT.mult))
        b.wait("vector", "XR", xl)
        par = t % 2
        if slot_store[par] > 0:
            b.wait("vector", "SO0" if par == 0 else "SO1", slot_store[par])
        vdummy(2)
        V(lambda e, par=par: e.tensor_add(
            gbufF[:, par * 2048:(par + 1) * 2048], xrl[:, :], ringF[:, :]))
        ov = vfence()
        prev_ov = ov
        b.wait("sync", "SV", ov)
        slot_store[par] = b.dma("sync", out_hbm[t * 128:(t + 1) * 128, :],
                                gbufF[:, par * 2048:(par + 1) * 2048],
                                "SO0" if par == 0 else "SO1")
    for sname in ("XI", "XA", "XB", "XR", "LT0", "LT1", "L2T", "KQ0", "KQ1",
                  "SP", "SO0", "SO1"):
        b.wait("sync", sname)
    for sname in [f"W{i}" for i in range(16)] + [f"AC{t}_{dq}" for t in range(8) for dq in range(4)]:
        b.wait("gpsimd", sname)

    nc.detect_race_conditions = False
    with nc.Block() as block:
        @block.sync
        def _(e):
            for fn in b.q["sync"]:
                fn(e)

        @block.gpsimd
        def _(e):
            for fn in b.q["gpsimd"]:
                fn(e)

        @block.tensor
        def _(e):
            for fn in b.q["tensor"]:
                fn(e)

        @block.vector
        def _(e):
            for fn in b.q["vector"]:
                fn(e)

        @block.scalar
        def _(e):
            for fn in b.q["scalar"]:
                fn(e)

    return nc


_NC_CACHE = {}


def _get_nc():
    if "nc" not in _NC_CACHE:
        _NC_CACHE["nc"] = build_program()
    return _NC_CACHE["nc"]


_TBL_CACHE = {}


def _host_tbl(half):
    if half in _TBL_CACHE:
        return _TBL_CACHE[half]
    import jax
    import jax.numpy as jnp
    cpu = jax.devices("cpu")[0]
    with jax.default_device(cpu):
        inv_freq = 1.0 / 10000.0 ** (jnp.arange(0, 128, 2, dtype=jnp.float32) / 128)
        pos = jnp.arange(2048, dtype=jnp.float32)
        freqs = pos[:, None] * inv_freq[None, :]
        cos = np.asarray(jnp.cos(freqs))
        sin = np.asarray(jnp.sin(freqs))
        lam = jnp.exp2(-8.0 * jnp.arange(1, 17, dtype=jnp.float32) / 16.0)
        r = jnp.arange(64, dtype=jnp.float32)
        diff = r[:, None] - r[None, :]
        diag = np.asarray(jnp.where(diff >= 0, jnp.exp(-lam[:, None, None] * diff), 0.0))
        qdec = np.asarray(jnp.exp(-lam[:, None] * (r + 1.0)))
        kdec = np.asarray(jnp.exp(-lam[:, None] * (64.0 - 1.0 - r)))
        blk = np.asarray(jnp.exp(-lam * 64.0))
    tblv = np.zeros((128, 6176), np.float32)
    p = np.arange(128)
    for t in range(8):
        own = half * 1024 + t * 128 + p
        pre = t * 128 + p
        tblv[:, t * 128:t * 128 + 64] = cos[own]
        tblv[:, t * 128 + 64:t * 128 + 128] = cos[own]
        tblv[:, 1024 + t * 128:1024 + t * 128 + 64] = sin[own]
        tblv[:, 1024 + t * 128 + 64:1024 + t * 128 + 128] = sin[own]
        tblv[:, 2048 + t * 128:2048 + t * 128 + 64] = cos[pre]
        tblv[:, 2048 + t * 128 + 64:2048 + t * 128 + 128] = cos[pre]
        tblv[:, 3072 + t * 128:3072 + t * 128 + 64] = sin[pre]
        tblv[:, 3072 + t * 128 + 64:3072 + t * 128 + 128] = sin[pre]
    for h in range(16):
        tblv[0:64, 4096 + h * 64:4096 + (h + 1) * 64] = diag[h].T
        tblv[64:128, 4096 + h * 64:4096 + (h + 1) * 64] = diag[h].T
        tblv[:, 5120 + h * 64:5120 + (h + 1) * 64] = qdec[h][None, :]
        tblv[:, 6144 + h] = kdec[h][p % 64]
        tblv[:, 6160 + h] = blk[h]
    tblv = np.ascontiguousarray(tblv)
    _TBL_CACHE[half] = tblv
    return tblv


def kernel(L0_attn_x, L0_attn_wq, L0_attn_wk, L0_attn_wv, L0_attn_wo,
           L0_ffn_w_gate, L0_ffn_w_up, L0_ffn_w_down):
    x = np.ascontiguousarray(np.asarray(L0_attn_x, dtype=np.float32))
    wq = np.asarray(L0_attn_wq, dtype=np.float32)
    wk = np.asarray(L0_attn_wk, dtype=np.float32)
    wv = np.asarray(L0_attn_wv, dtype=np.float32)
    wo = np.asarray(L0_attn_wo, dtype=np.float32)
    wg = np.asarray(L0_ffn_w_gate, dtype=np.float32)
    wu = np.asarray(L0_ffn_w_up, dtype=np.float32)
    wd = np.asarray(L0_ffn_w_down, dtype=np.float32)

    wqT = np.ascontiguousarray(wq.T)
    wkT = np.ascontiguousarray(wk.T)
    wvT = np.ascontiguousarray(wv.T)
    woT = np.ascontiguousarray(wo.T)
    wgP = np.ascontiguousarray(
        wg.reshape(64, 128, 16, 128).transpose(0, 3, 2, 1).reshape(64, 128, 2048))
    wuP = np.ascontiguousarray(
        wu.reshape(64, 128, 16, 128).transpose(0, 3, 2, 1).reshape(64, 128, 2048))
    wdT = np.ascontiguousarray(wd.T)

    zeros = np.zeros((T, D), np.float32)
    in_maps = []
    for core in range(8):
        bb, half = core // 2, core % 2
        xs = np.ascontiguousarray(x[bb, half * T:(half + 1) * T, :])
        xp = np.ascontiguousarray(x[bb, 0:T, :]) if half == 1 else zeros
        in_maps.append(dict(
            x_own=xs, x_pre=xp, wqT=wqT, wkT=wkT, wvT=wvT, woT=woT,
            wgP=wgP, wuP=wuP, wdT=wdT, tbl=_host_tbl(half)))

    nc = _get_nc()
    res = run_bass_kernel_spmd(nc, in_maps, core_ids=list(range(8)))
    out = np.empty((B, S, D), np.float32)
    for core in range(8):
        bb, half = core // 2, core % 2
        out[bb, half * T:(half + 1) * T, :] = res.results[core]["out"]
    return out


# version 10
# speedup vs baseline: 1.1189x; 1.0164x over previous
import numpy as np
import concourse.bass as bass
import concourse.mybir as mybir
from concourse.bass_utils import run_bass_kernel_spmd

F32 = mybir.dt.float32
BF16 = mybir.dt.bfloat16
AOT = mybir.AluOpType
AFT = mybir.ActivationFunctionType
AXX = mybir.AxisListType.X

B, S, D = 4, 2048, 2048
HID = 8192
T, NT, NKC, NCH = 1024, 8, 16, 8
EPS = 1e-5

ENGS = ("sync", "gpsimd", "tensor", "vector", "scalar")
DEBUG = False
DEBUG_BARRIER = True
BARRIERS = set()


class Bld:
    def __init__(self, nc):
        self.nc = nc
        self.q = {e: [] for e in ENGS}
        self.sems = {}
        self.cnt = {}
        names = ["SX", "SW", "ST", "SV", "SS", "SG", "TB", "XI", "XA", "XB", "XR",
                 "LT0", "LT1", "L2T", "KQ0", "KQ1", "SP", "SO0", "SO1", "AC"]
        names += [f"W{i}" for i in range(16)]
        names += [f"AC{t}_{dq}" for t in range(8) for dq in range(4)]
        for s in names:
            self.sems[s] = nc.alloc_semaphore(s)
            self.cnt[s] = 0

    def emit(self, eng, fn):
        self.q[eng].append(fn)

    def wait(self, eng, sem, val=None):
        v = self.cnt[sem] if val is None else val
        if v <= 0:
            return
        s = self.sems[sem]
        self.emit(eng, lambda e, s=s, v=v: e.wait_ge(s, v))

    def op(self, eng, fn, inc=None, amt=1):
        if inc is None:
            self.emit(eng, fn)
            return None
        s = self.sems[inc]
        self.cnt[inc] += amt
        self.emit(eng, lambda e, s=s, a=amt, f=fn: f(e).then_inc(s, a))
        return self.cnt[inc]

    def dma(self, eng, out_ap, in_ap, inc="SW", accum=False):
        s = self.sems[inc]
        self.cnt[inc] += 16
        if accum:
            self.emit(eng, lambda e, o=out_ap, i=in_ap, s=s:
                      e.dma_start(o, i, accum_op=AOT.add).then_inc(s, 16))
        else:
            self.emit(eng, lambda e, o=out_ap, i=in_ap, s=s:
                      e.dma_start(o, i).then_inc(s, 16))
        return self.cnt[inc]

    def dmat(self, out_ap, in_ap, inc="SX"):
        s = self.sems[inc]
        self.cnt[inc] += 16
        self.emit("sync", lambda e, o=out_ap, i=in_ap, s=s:
                  e.dma_start_transpose(o, i).then_inc(s, 16))
        return self.cnt[inc]


def AP(t, off, dims):
    h = t.tensor if hasattr(t, "tensor") else t
    return bass.AP(h, off, dims)


def build_program():
    nc = bass.Bass("TRN2", target_bir_lowering=False)

    x_own = nc.dram_tensor("x_own", [T, D], F32, kind="ExternalInput").ap()
    x_pre = nc.dram_tensor("x_pre", [T, D], F32, kind="ExternalInput").ap()
    wqT = nc.dram_tensor("wqT", [D, 2048], F32, kind="ExternalInput").ap()
    wkT = nc.dram_tensor("wkT", [D, 512], F32, kind="ExternalInput").ap()
    wvT = nc.dram_tensor("wvT", [D, 512], F32, kind="ExternalInput").ap()
    woT = nc.dram_tensor("woT", [2048, D], F32, kind="ExternalInput").ap()
    wgP = nc.dram_tensor("wgP", [64, 128, 2048], F32, kind="ExternalInput").ap()
    wuP = nc.dram_tensor("wuP", [64, 128, 2048], F32, kind="ExternalInput").ap()
    wdT = nc.dram_tensor("wdT", [HID, D], F32, kind="ExternalInput").ap()
    tbl = nc.dram_tensor("tbl", [128, 6176], F32, kind="ExternalInput").ap()
    out_hbm = nc.dram_tensor("out", [T, D], F32, kind="ExternalOutput").ap()
    x1_hbm = nc.dram_tensor("x1spill", [T, D], F32, kind="Internal").ap()

    SB = 16640
    _A = nc.alloc_sbuf_tensor_at
    A = lambda name, shape, dt, offset: _A(name, shape, dt, offset=SB + offset)
    fring = A("fring", [128, 4096], F32, offset=0)
    lnT = A("lnT", [128, 16384], BF16, offset=16384)
    gbuf = A("gbuf", [128, 16384], BF16, offset=49152)
    gbufF = A("gbufF", [128, 4096], F32, offset=49152)
    tgF = A("tgF", [128, 4096], F32, offset=65536)
    qoT = A("qoT", [128, 16384], BF16, offset=81920)
    kT = A("kT", [128, 4096], BF16, offset=114688)
    ktok = A("ktok", [128, 4096], BF16, offset=122880)
    vtok = A("vtok", [128, 4096], BF16, offset=131072)
    KVa = A("KVa", [128, 2048], F32, offset=139264)
    KVb = A("KVb", [128, 2048], F32, offset=147456)
    acc = A("acc", [128, 16384], F32, offset=81920)
    aoF1 = A("aoF1", [128, 6144], F32, offset=114688)
    stage2 = A("stage2", [128, 2048], BF16, offset=139264)
    xrl = A("xrl", [128, 2048], F32, offset=147456)
    KVbf = A("KVbf", [128, 2048], BF16, offset=155648)
    warena = A("warena", [128, 8192], BF16, offset=159744)
    ring = A("ring", [128, 4096], BF16, offset=176128)
    ringF = A("ringF", [128, 2048], F32, offset=176128)
    scos = A("scos", [128, 1024], BF16, offset=184320)
    ssin = A("ssin", [128, 1024], BF16, offset=186368)
    sm0 = 188416
    stat = A("stat", [128, 64], F32, offset=sm0)
    lamF = A("lamF", [128, 80], F32, offset=sm0 + 256)
    idxS = A("idxS", [128, 256], F32, offset=sm0 + 576)
    ktil = A("ktil", [128, 512], BF16, offset=sm0 + 1600)
    qtil = A("qtil", [128, 512], BF16, offset=sm0 + 2624)
    scm = A("scm", [128, 1024], BF16, offset=sm0 + 3648)
    dumV = A("dumV", [128, 16], F32, offset=sm0 + 5696)
    sqd = A("sqd", [128, 2048], BF16, offset=sm0 + 5888)
    scm2 = A("scm2", [128, 1024], BF16, offset=sm0 + 9984)
    dumB = A("dumB", [128, 128], F32, offset=sm0 + 12032)
    ktil2 = A("ktil2", [128, 512], BF16, offset=sm0 + 12544)
    qtil2 = A("qtil2", [128, 512], BF16, offset=sm0 + 13568)

    p0 = nc.alloc_psum_tensor("p0", [128, 2048], F32)
    p1 = nc.alloc_psum_tensor("p1", [128, 2048], F32)
    PS = (p0, p1)

    b = Bld(nc)

    def V(fn, inc=None):
        return b.op("vector", fn, inc=inc)

    def Sc(fn, inc=None):
        return b.op("scalar", fn, inc=inc)

    def Tt(fn, inc=None):
        return b.op("tensor", fn, inc=inc)

    def vdummy(n=2):
        for _ in range(n):
            V(lambda e: e.tensor_copy(dumB[:, 0:64], dumB[:, 64:128]))

    def vfence(inc="SV"):
        vdummy(2)
        return V(lambda e: e.tensor_copy(dumB[:, 0:64], dumB[:, 64:128]), inc=inc)

    def sfence(inc="SS"):
        Sc(lambda e: e.copy(dumV[:, 0:1], dumV[:, 4:5]))
        return Sc(lambda e: e.copy(dumV[:, 1:2], dumV[:, 5:6]), inc=inc)

    dbg_outs = []

    def _dbg(name, src_ap, shape):
        if not DEBUG:
            return
        d = nc.dram_tensor("dbg_" + name, list(shape), F32, kind="ExternalOutput").ap()
        for sname in list(b.sems):
            if sname != "SW":
                b.wait("gpsimd", sname)
        b.dma("gpsimd", d[:, :], src_ap, "SW")
        if DEBUG_BARRIER:
            for eng in ENGS:
                b.wait(eng, "SW")
        dbg_outs.append(name)

    def _bar(pid):
        if pid not in BARRIERS:
            return
        for sname in list(b.sems):
            b.wait("gpsimd", sname)
        b.op("gpsimd", lambda e: e.memset(dumV[:, 8:9], 0.0), inc="SG", amt=1)
        for eng in ENGS:
            b.wait(eng, "SW")
            b.wait(eng, "SG")

    SQD = 8192

    def tg(c0, w=1):
        return tgF[:, c0:c0 + w]

    b.op("gpsimd", lambda e: e.memset(KVa[:, :], 0.0), inc="SG", amt=1)
    b.op("gpsimd", lambda e: e.memset(stat[:, :], 0.0), inc="SG", amt=1)
    b.op("gpsimd", lambda e: e.memset(dumV[:, :], 0.0), inc="SG", amt=1)
    b.op("gpsimd", lambda e: e.memset(dumB[:, :], 0.0), inc="SG", amt=1)
    for _eng in ("vector", "scalar", "tensor", "sync"):
        b.wait(_eng, "SG")
    b.dma("gpsimd", ring[:, 0:1024], tbl[:, 4096:5120], "TB")
    b.dma("gpsimd", ring[:, 1024:2048], tbl[:, 5120:6144], "TB")
    b.dma("gpsimd", lamF[:, 32:48], tbl[:, 6144:6160], "TB")
    b.dma("gpsimd", lamF[:, 48:64], tbl[:, 6160:6176], "TB")
    tb1 = b.dma("gpsimd", scos[:, :], tbl[:, 2048:3072], "TB")
    tb1 = b.dma("gpsimd", ssin[:, :], tbl[:, 3072:4096], "TB")
    b.wait("vector", "TB", tb1)

    def ln_stats(src_tile, jc, sq_wait_sv=None):
        V(lambda e, s=src_tile, jc=jc: e.reduce_sum(stat[:, jc:jc + 1], s, AXX))
        if sq_wait_sv is not None:
            b.wait("scalar", "SV", sq_wait_sv)
        ss = Sc(lambda e, s=src_tile, jc=jc: e.activation(
            sqd[:, :], s, AFT.Square, accum_out=stat[:, 8 + jc:9 + jc]),
            inc="SS")
        b.wait("vector", "SS", ss)
        vdummy(2)
        V(lambda e, jc=jc: e.tensor_scalar_mul(stat[:, 16 + jc:17 + jc], stat[:, jc:jc + 1], 1.0 / 2048.0))
        vdummy(2)
        V(lambda e, jc=jc: e.tensor_mul(stat[:, 40 + jc:41 + jc], stat[:, 16 + jc:17 + jc], stat[:, 16 + jc:17 + jc]))
        V(lambda e, jc=jc: e.tensor_scalar_mul(stat[:, 48 + jc:49 + jc], stat[:, 8 + jc:9 + jc], 1.0 / 2048.0))
        vdummy(2)
        V(lambda e, jc=jc: e.tensor_sub(stat[:, 56 + jc:57 + jc], stat[:, 48 + jc:49 + jc], stat[:, 40 + jc:41 + jc]))
        vdummy(2)
        vv = V(lambda e, jc=jc: e.tensor_scalar_add(stat[:, 40 + jc:41 + jc], stat[:, 56 + jc:57 + jc], EPS), inc="SV")
        b.wait("scalar", "SV", vv)
        ss = Sc(lambda e, jc=jc: e.activation(stat[:, 48 + jc:49 + jc], stat[:, 40 + jc:41 + jc], AFT.Sqrt), inc="SS")
        b.wait("vector", "SS", ss)
        V(lambda e, jc=jc: e.reciprocal(stat[:, 24 + jc:25 + jc], stat[:, 48 + jc:49 + jc]))
        vdummy(2)
        V(lambda e, jc=jc: e.tensor_mul(stat[:, 56 + jc:57 + jc], stat[:, 16 + jc:17 + jc], stat[:, 24 + jc:25 + jc]))
        vdummy(2)
        return V(lambda e, jc=jc: e.tensor_scalar_mul(stat[:, 32 + jc:33 + jc], stat[:, 56 + jc:57 + jc], -1.0), inc="SV")

    def ln_apply(src_tile, dst_tile, jc, sv):
        b.wait("scalar", "SV", sv)
        Sc(lambda e, s=src_tile, d=dst_tile, jc=jc: e.activation(
            d, s, AFT.Identity, bias=stat[:, 32 + jc:33 + jc], scale=stat[:, 24 + jc:25 + jc]))
        return sfence()

    def ln_phase(x_src):
        applies = {}
        transposes = {}
        for j in range(NT):
            par = j % 2
            xsem = "XA" if par == 0 else "XB"
            ltsem = "LT0" if par == 0 else "LT1"
            slot = fring[:, par * 2048:par * 2048 + 2048]
            if j >= 2:
                b.wait("sync", "SS", applies[j - 2])
            ld = b.dma("sync", slot, x_src[j * 128:(j + 1) * 128, :], xsem)
            b.wait("vector", xsem, ld)
            b.wait("scalar", xsem, ld)
            sv = ln_stats(slot, j % 8)
            stg = gbuf[:, par * 2048:par * 2048 + 2048]
            if j >= 2:
                b.wait("scalar", ltsem, transposes[j - 2])
            a = ln_apply(slot, stg, j % 8, sv)
            applies[j] = a
            b.wait("sync", "SS", a)
            transposes[j] = b.dmat(AP(lnT, j * 128, [[16384, 128], [1024, 16], [1, 128]]), stg, inc=ltsem)

    wslot_state = {"next": 0}
    slot_free_st = [0] * 16

    def wslot():
        i = wslot_state["next"]
        wslot_state["next"] = (i + 1) % 16
        return i

    def kouter_pass(lhs_fn, w_src_fn, evac_fn, pre_waits=()):
        slots, loads = [], []
        for k in range(NKC):
            i = wslot()
            if slot_free_st[i] > 0:
                b.wait("gpsimd", "ST", slot_free_st[i])
            sl = warena[:, i * 512:(i + 1) * 512]
            slots.append((i, sl))
            loads.append(b.dma("gpsimd", sl, w_src_fn(k), f"W{i}"))
        b.wait("tensor", "SV")
        b.wait("tensor", "SS")
        for sem in pre_waits:
            b.wait("tensor", sem)
        for k in range(NKC):
            b.wait("tensor", f"W{slots[k][0]}", loads[k])
            for t in range(NT):
                reg = PS[t // 4][:, (t % 4) * 512:(t % 4) * 512 + 512]
                lhs = lhs_fn(k, t)
                if t == NT - 1:
                    st = Tt(lambda e, reg=reg, lhs=lhs, sl=slots[k][1], k=k: e.matmul(
                        reg, lhs, sl, start=(k == 0), stop=(k == NKC - 1)), inc="ST")
                    slot_free_st[slots[k][0]] = st
                else:
                    Tt(lambda e, reg=reg, lhs=lhs, sl=slots[k][1], k=k: e.matmul(
                        reg, lhs, sl, start=(k == 0), stop=(k == NKC - 1)))
        stlast = b.cnt["ST"]
        for t in range(NT):
            reg = PS[t // 4][:, (t % 4) * 512:(t % 4) * 512 + 512]
            evac_fn(t, reg, stlast)

    def ln_lhs(k, t):
        return AP(lnT, k * 1024 + t * 128, [[16384, 128], [1, 128]])

    def emit_rope(psrc, dst, t, nh):
        pT, po = psrc.tensor, psrc.offset
        dstT, do = dst.tensor, dst.offset
        cos_lo = AP(scos, t * 128, [[1024, 128], [0, nh], [1, 64]])
        sin_lo = AP(ssin, t * 128, [[1024, 128], [0, nh], [1, 64]])
        q_lo = bass.AP(pT, po, [[2048, 128], [128, nh], [1, 64]])
        q_hi = bass.AP(pT, po + 64, [[2048, 128], [128, nh], [1, 64]])
        o_lo = bass.AP(dstT, do, [[16384, 128], [128, nh], [1, 64]])
        o_hi = bass.AP(dstT, do + 64, [[16384, 128], [128, nh], [1, 64]])
        m1 = AP(tgF, 96, [[4096, 128], [64, nh], [1, 64]])
        m2 = AP(tgF, 608, [[4096, 128], [64, nh], [1, 64]])
        m3 = AP(tgF, 1120, [[4096, 128], [64, nh], [1, 64]])
        m4 = AP(tgF, 1632, [[4096, 128], [64, nh], [1, 64]])
        V(lambda e: e.tensor_mul(m1, q_lo, cos_lo))
        V(lambda e: e.tensor_mul(m2, q_hi, sin_lo))
        V(lambda e: e.tensor_mul(m3, q_hi, cos_lo))
        V(lambda e: e.tensor_mul(m4, q_lo, sin_lo))
        vdummy(2)
        V(lambda e: e.tensor_sub(o_lo, m1, m2))
        vdummy(2)
        V(lambda e: e.tensor_add(o_hi, m3, m4))
        vdummy(4)

    stage_free = {0: 0, 1: 0}

    def evac_k(to_kT):
        def f(t, reg, st):
            b.wait("vector", "ST", st)
            par = t % 2
            stB = gbuf[:, 6144 + par * 1024: 6144 + par * 1024 + 512]
            if stage_free[par] > 0:
                b.wait("vector", "KQ0" if par == 0 else "KQ1", stage_free[par])
            emit_rope(reg, stB, t, 4)
            V(lambda e, t=t, stB=stB: e.tensor_copy(
                AP(ktok, t * 512, [[4096, 128], [1, 512]]), stB))
            kv = vfence()
            if to_kT:
                b.wait("sync", "SV", kv)
                stage_free[par] = b.dmat(
                    AP(kT, t * 128, [[4096, 128], [1024, 4], [1, 128]]), stB,
                    inc="KQ0" if par == 0 else "KQ1")
        return f

    def evac_v(t, reg, st):
        b.wait("vector", "ST", st)
        V(lambda e, reg=reg, t=t: e.tensor_copy(
            AP(vtok, t * 512, [[4096, 128], [1, 512]]), reg),
          inc="SV" if t == NT - 1 else None)

    def evac_q(qq):
        def f(t, reg, st):
            b.wait("vector", "ST", st)
            par = t % 2
            stB = gbuf[:, 6144 + par * 1024: 6144 + par * 1024 + 512]
            if stage_free[par] > 0:
                b.wait("vector", "KQ0" if par == 0 else "KQ1", stage_free[par])
            emit_rope(reg, stB, t, 4)
            qv = vfence()
            b.wait("sync", "SV", qv)
            stage_free[par] = b.dmat(
                AP(qoT, (4 * qq) * 1024 + t * 128, [[16384, 128], [1024, 4], [1, 128]]), stB,
                inc="KQ0" if par == 0 else "KQ1")
        return f

    def kv_scan(with_output):
        groups = [(c64, g) for c64 in range(16) for g in range(4)]
        KT = (ktil, ktil2)
        QT = (qtil, qtil2)
        last_a = [0, 0]
        smm = {}
        vm = {}
        vprep = {}

        def emitA(s):
            c64, g = groups[s]
            par = s % 2
            t, hh = c64 // 2, c64 % 2
            po = hh * 64
            kt_, qt_ = KT[par], QT[par]
            if last_a[par] > 0:
                b.wait("vector", "ST", last_a[par])
            for hl in range(4):
                h = 4 * g + hl
                V(lambda e, t=t, po=po, g=g, hl=hl, h=h, kt_=kt_: e.tensor_scalar(
                    kt_[po:po + 64, hl * 128:(hl + 1) * 128],
                    ktok[po:po + 64, t * 512 + g * 128:t * 512 + (g + 1) * 128],
                    lamF[po:po + 64, 32 + h:33 + h], None, AOT.mult))
            if with_output:
                for hl in range(4):
                    h = 4 * g + hl
                    V(lambda e, t=t, po=po, h=h, hl=hl, qt_=qt_: e.tensor_mul(
                        qt_[:, hl * 64:(hl + 1) * 64],
                        AP(qoT, h * 1024 + t * 128 + po, [[16384, 128], [1, 64]]),
                        ring[:, 1024 + h * 64:1024 + (h + 1) * 64]))
            vprep[s] = V(lambda e: e.tensor_copy(dumV[:, 1:2], dumV[:, 5:6]), inc="SV")
            b.wait("tensor", "SV", vprep[s])
            if with_output:
                pp = PS[par]
                smm[s] = Tt(lambda e, t=t, po=po, g=g, pp=pp: e.matmul(
                    pp[po:po + 64, 1024:1280],
                    kT[:, g * 1024 + t * 128 + po: g * 1024 + t * 128 + po + 64],
                    AP(qoT, (4 * g) * 1024 + t * 128 + po, [[16384, 128], [1024, 4], [1, 64]]),
                    start=True, stop=True), inc="ST")

        def emitB(s):
            c64, g = groups[s]
            par = s % 2
            t, hh = c64 // 2, c64 % 2
            po = hh * 64
            pp = PS[par]
            kt_, qt_ = KT[par], QT[par]
            src = (KVa, KVb)[c64 % 2]
            dst = (KVb, KVa)[c64 % 2]
            if with_output:
                b.wait("vector", "ST", smm[s])
                scm_sl = scm[po:po + 64, par * 256:par * 256 + 256]
                vm[s] = V(lambda e, g=g, po=po, pp=pp, scm_sl=scm_sl: e.tensor_mul(
                    scm_sl, pp[po:po + 64, 1024:1280],
                    ring[po:po + 64, (4 * g) * 64:(4 * g + 4) * 64]), inc="SV")
                b.wait("tensor", "SV", vm[s])
                b.wait("tensor", "SS")
                for hl in range(4):
                    h = 4 * g + hl
                    Tt(lambda e, t=t, po=po, g=g, hl=hl, pp=pp, par=par: e.matmul(
                        pp[:, 1536 + hl * 64:1536 + (hl + 1) * 64],
                        vtok[po:po + 64, t * 512 + g * 128:t * 512 + (g + 1) * 128],
                        scm[po:po + 64, par * 256 + hl * 64:par * 256 + (hl + 1) * 64],
                        start=True, stop=False))
                    Tt(lambda e, h=h, hl=hl, pp=pp, qt_=qt_: e.matmul(
                        pp[:, 1536 + hl * 64:1536 + (hl + 1) * 64],
                        KVbf[:, h * 128:(h + 1) * 128],
                        qt_[:, hl * 64:(hl + 1) * 64],
                        start=False, stop=True))
            for hl in range(4):
                st = Tt(lambda e, t=t, po=po, g=g, hl=hl, pp=pp, kt_=kt_: e.matmul(
                    pp[:, 512 + hl * 128:512 + (hl + 1) * 128],
                    kt_[po:po + 64, hl * 128:(hl + 1) * 128],
                    vtok[po:po + 64, t * 512 + g * 128:t * 512 + (g + 1) * 128],
                    start=True, stop=True),
                    inc="ST" if hl == 3 else None)
            last_a[par] = st
            b.wait("vector", "ST", st)
            if with_output:
                V(lambda e, t=t, po=po, g=g, pp=pp: e.tensor_copy(
                    AP(qoT, (4 * g) * 1024 + t * 128 + po, [[16384, 128], [1024, 4], [1, 64]]),
                    pp[:, 1536:1792]))
            for hl in range(4):
                h = 4 * g + hl
                V(lambda e, h=h, hl=hl, src=src, dst=dst, pp=pp: e.scalar_tensor_tensor(
                    dst[:, h * 128:(h + 1) * 128],
                    src[:, h * 128:(h + 1) * 128],
                    lamF[:, 48 + h:49 + h],
                    pp[:, 512 + hl * 128:512 + (hl + 1) * 128],
                    AOT.mult, AOT.add))
            vu = V(lambda e: e.tensor_copy(dumV[:, 2:3], dumV[:, 6:7]), inc="SV")
            if with_output and c64 < 15:
                b.wait("scalar", "SV", vu)
                for hl in range(4):
                    h = 4 * g + hl
                    Sc(lambda e, h=h, dst=dst: e.copy(
                        KVbf[:, h * 128:(h + 1) * 128], dst[:, h * 128:(h + 1) * 128]),
                        inc="SS" if hl == 3 else None)

        for s in range(len(groups)):
            emitA(s)
            if s >= 1:
                emitB(s - 1)
        emitB(len(groups) - 1)

    _dbg("scos", scos[:, :], (128, 1024))
    _dbg("ssin", ssin[:, :], (128, 1024))
    _dbg("diag", ring[:, 0:2048], (128, 2048))
    _dbg("qdec", ring[:, 2048:4096], (128, 2048))
    _dbg("lam", lamF[:, 0:64], (128, 64))
    ln_phase(x_pre)
    _bar(2)
    _dbg("lnTpre", lnT[:, :], (128, 16384))
    kouter_pass(ln_lhs, lambda k: wkT[k * 128:(k + 1) * 128, 0:512], evac_k(False))
    kouter_pass(ln_lhs, lambda k: wvT[k * 128:(k + 1) * 128, 0:512], evac_v)
    b.wait("tensor", "SV")
    b.wait("vector", "SW")
    _dbg("ktokpre", ktok[:, :], (128, 4096))
    _dbg("vtokpre", vtok[:, :], (128, 4096))
    kv_scan(with_output=False)
    _bar(3)
    _dbg("KV0", KVa[:, :], (128, 2048))

    b.wait("gpsimd", "SV")
    tb2 = b.dma("gpsimd", scos[:, :], tbl[:, 0:1024], "TB")
    tb2 = b.dma("gpsimd", ssin[:, :], tbl[:, 1024:2048], "TB")
    b.wait("vector", "TB", tb2)
    _dbg("scos0", scos[:, :], (128, 1024))
    ln_phase(x_own)
    _bar(4)
    _dbg("lnT1", lnT[:, :], (128, 16384))
    kouter_pass(ln_lhs, lambda k: wkT[k * 128:(k + 1) * 128, 0:512], evac_k(True))
    kouter_pass(ln_lhs, lambda k: wvT[k * 128:(k + 1) * 128, 0:512], evac_v)
    for qq in range(4):
        kouter_pass(ln_lhs,
                    lambda k, qq=qq: wqT[k * 128:(k + 1) * 128, qq * 512:(qq + 1) * 512],
                    evac_q(qq))
    _bar(5)
    _dbg("qT", qoT[:, :], (128, 16384))
    _dbg("kTd", kT[:, :], (128, 4096))
    _dbg("ktok1", ktok[:, :], (128, 4096))
    _dbg("vtok1", vtok[:, :], (128, 4096))

    b.wait("tensor", "KQ0")
    b.wait("tensor", "KQ1")
    b.wait("vector", "KQ0")
    b.wait("vector", "KQ1")
    b.wait("scalar", "SV")
    for h in range(16):
        Sc(lambda e, h=h: e.copy(KVbf[:, h * 128:(h + 1) * 128], KVa[:, h * 128:(h + 1) * 128]),
            inc="SS" if h == 15 else None)
    kv_scan(with_output=True)
    _bar(6)
    _dbg("oT", qoT[:, :], (128, 16384))

    def ao_base(t):
        if t < 3:
            return (aoF1, t * 2048, 6144)
        if t < 5:
            return (fring, (t - 3) * 2048, 4096)
        if t < 7:
            return (gbufF, (t - 5) * 2048, 4096)
        return (tgF, 2048, 4096)

    def ao_tile(t):
        tt, off, pitch = ao_base(t)
        return AP(tt, off, [[pitch, 128], [1, 2048]])

    def ao_slice(t, dq):
        tt, off, pitch = ao_base(t)
        return AP(tt, off + dq * 512, [[pitch, 128], [1, 512]])

    def evac_ao(dq):
        def f(t, reg, st):
            b.wait("vector", "ST", st)
            V(lambda e, reg=reg, t=t, dq=dq: e.tensor_copy(ao_slice(t, dq), reg),
              inc="SV" if t == NT - 1 else None)
        return f

    for dq in range(4):
        kouter_pass(
            lambda k, t: AP(qoT, k * 1024 + t * 128, [[16384, 128], [1, 128]]),
            lambda k, dq=dq: woT[k * 128:(k + 1) * 128, dq * 512:(dq + 1) * 512],
            evac_ao(dq), pre_waits=("KQ0", "KQ1"))

    _bar(7)
    _dbg("ao0", ao_tile(0), (128, 2048))
    _dbg("ao7", ao_tile(7), (128, 2048))
    b.wait("sync", "SV")
    prev_x1v = 0
    prev_store = 0
    for t in range(NT):
        at = ao_tile(t)
        sv = ln_stats(at, t % 8, sq_wait_sv=b.cnt["SV"])
        if prev_x1v > 0:
            b.wait("sync", "SV", prev_x1v)
        xl = b.dma("sync", xrl[:, :], x_own[t * 128:(t + 1) * 128, :], "XR")
        b.wait("vector", "SV", sv)
        V(lambda e, at=at, jc=t % 8: e.tensor_scalar(
            ringF[:, :], at, stat[:, 16 + jc:17 + jc], stat[:, 24 + jc:25 + jc],
            AOT.subtract, AOT.mult))
        b.wait("vector", "XR", xl)
        vdummy(2)
        V(lambda e, at=at: e.tensor_add(at, xrl[:, :], ringF[:, :]))
        x1v = vfence()
        prev_x1v = x1v
        b.wait("sync", "SV", x1v)
        b.dma("sync", x1_hbm[t * 128:(t + 1) * 128, :], at, "SP")
        vdummy(2)
        sv2 = ln_stats(at, t % 8, sq_wait_sv=x1v)
        if prev_store > 0:
            b.wait("scalar", "L2T", prev_store)
        a2 = ln_apply(at, stage2[:, :], t % 8, sv2)
        b.wait("sync", "SS", a2)
        prev_store = b.dmat(AP(lnT, t * 128, [[16384, 128], [1024, 16], [1, 128]]), stage2[:, :],
                            inc="L2T")

    _bar(8)
    _dbg("lnT2", lnT[:, :], (128, 16384))
    b.wait("tensor", "L2T")
    b.wait("vector", "SP")
    dstage_free = [None, None, None, None]
    acc_sw = [[0] * 4 for _ in range(NT)]

    for sub in range(4):
        last_gm = 0
        for ht16 in range(16):
            ht = sub * 16 + ht16
            par = ht16 % 2
            pp = PS[par]
            bg, bu = par * 2, par * 2 + 1
            wmax = max(slot_free_st[bg * 4: bg * 4 + 4] + slot_free_st[bu * 4: bu * 4 + 4])
            if wmax > 0:
                b.wait("gpsimd", "ST", wmax)
            ldg = b.dma("gpsimd", warena[:, bg * 2048:(bg + 1) * 2048], wgP[ht, :, :], f"W{bg * 4}")
            ldu = b.dma("gpsimd", warena[:, bu * 2048:(bu + 1) * 2048], wuP[ht, :, :], f"W{bu * 4}")
            b.wait("tensor", "SS")
            b.wait("tensor", "SV")
            b.wait("tensor", f"W{bg * 4}", ldg)
            b.wait("tensor", f"W{bu * 4}", ldu)
            st = None
            for k in range(NKC):
                for half in range(2):
                    lhs_g = warena[:, bg * 2048 + k * 128: bg * 2048 + (k + 1) * 128]
                    lhs_u = warena[:, bu * 2048 + k * 128: bu * 2048 + (k + 1) * 128]
                    rhs = lnT[:, k * 1024 + half * 512: k * 1024 + half * 512 + 512]
                    Tt(lambda e, pp=pp, half=half, lhs_g=lhs_g, rhs=rhs, k=k: e.matmul(
                        pp[:, half * 512:half * 512 + 512], lhs_g, rhs,
                        start=(k == 0), stop=(k == NKC - 1)))
                    st = Tt(lambda e, pp=pp, half=half, lhs_u=lhs_u, rhs=rhs, k=k: e.matmul(
                        pp[:, 1024 + half * 512:1024 + half * 512 + 512], lhs_u, rhs,
                        start=(k == 0), stop=(k == NKC - 1)),
                        inc="ST" if (k == NKC - 1 and half == 1) else None)
            for i in list(range(bg * 4, bg * 4 + 4)) + list(range(bu * 4, bu * 4 + 4)):
                slot_free_st[i] = st
            b.wait("scalar", "ST", st)
            ss = Sc(lambda e, pp=pp: e.activation(scm[:, :], pp[:, 0:1024], AFT.Silu), inc="SS")
            b.wait("vector", "SS", ss)
            last_gm = V(lambda e, pp=pp, ht16=ht16: e.tensor_mul(
                gbuf[:, ht16 * 1024:(ht16 + 1) * 1024], scm[:, :], pp[:, 1024:2048]), inc="SV")
        b.wait("tensor", "SV", last_gm)
        if sub == 0:
            _dbg("g0", gbuf[:, :], (128, 16384))
        for dq in range(4):
            slots, loads = [], []
            for k in range(NKC):
                i = wslot()
                if slot_free_st[i] > 0:
                    b.wait("gpsimd", "ST", slot_free_st[i])
                sl = warena[:, i * 512:(i + 1) * 512]
                slots.append((i, sl))
                loads.append(b.dma("gpsimd", sl,
                                   wdT[sub * 2048 + k * 128: sub * 2048 + (k + 1) * 128,
                                       dq * 512:(dq + 1) * 512], f"W{i}"))
            b.wait("tensor", "SV")
            for k in range(NKC):
                b.wait("tensor", f"W{slots[k][0]}", loads[k])
                for t in range(NT):
                    reg = PS[t // 4][:, (t % 4) * 512:(t % 4) * 512 + 512]
                    lhs = AP(gbuf, k * 1024 + t * 128, [[16384, 128], [1, 128]])
                    if t == NT - 1:
                        st = Tt(lambda e, reg=reg, lhs=lhs, sl=slots[k][1], k=k: e.matmul(
                            reg, lhs, sl, start=(k == 0), stop=(k == NKC - 1)), inc="ST")
                        slot_free_st[slots[k][0]] = st
                    else:
                        Tt(lambda e, reg=reg, lhs=lhs, sl=slots[k][1], k=k: e.matmul(
                            reg, lhs, sl, start=(k == 0), stop=(k == NKC - 1)))
            stlast = b.cnt["ST"]
            b.wait("vector", "ST", stlast)
            for t in range(NT):
                reg = PS[t // 4][:, (t % 4) * 512:(t % 4) * 512 + 512]
                dsl = acc[:, t * 2048 + dq * 512: t * 2048 + dq * 512 + 512]
                if sub == 0:
                    V(lambda e, reg=reg, dsl=dsl: e.tensor_copy(dsl, reg),
                      inc="SV" if t == NT - 1 else None)
                else:
                    spar = t % 4
                    if dstage_free[spar] is not None:
                        b.wait("vector", *dstage_free[spar])
                    stg = ringF[:, spar * 512:spar * 512 + 512]
                    V(lambda e, reg=reg, stg=stg: e.tensor_copy(stg, reg))
                    vv = vfence()
                    b.wait("gpsimd", "SV", vv)
                    acsem = f"AC{t}_{dq}"
                    if acc_sw[t][dq] > 0:
                        b.wait("gpsimd", acsem, acc_sw[t][dq])
                    sw = b.dma("gpsimd", dsl, stg, acsem, accum=True)
                    dstage_free[spar] = (acsem, sw)
                    acc_sw[t][dq] = sw

    _bar(9)
    _dbg("accd", acc[:, :], (128, 16384))
    for t in range(8):
        for dq in range(4):
            b.wait("vector", f"AC{t}_{dq}")
            b.wait("scalar", f"AC{t}_{dq}")
    b.wait("scalar", "SV")
    b.wait("sync", "SP")
    prev_ov = 0
    slot_store = [0, 0]
    for t in range(NT):
        at = acc[:, t * 2048:(t + 1) * 2048]
        sv = ln_stats(at, t % 8, sq_wait_sv=b.cnt["SV"])
        if prev_ov > 0:
            b.wait("sync", "SV", prev_ov)
        xl = b.dma("sync", xrl[:, :], x1_hbm[t * 128:(t + 1) * 128, :], "XR")
        b.wait("vector", "SV", sv)
        V(lambda e, at=at, jc=t % 8: e.tensor_scalar(
            ringF[:, :], at, stat[:, 16 + jc:17 + jc], stat[:, 24 + jc:25 + jc],
            AOT.subtract, AOT.mult))
        b.wait("vector", "XR", xl)
        par = t % 2
        if slot_store[par] > 0:
            b.wait("vector", "SO0" if par == 0 else "SO1", slot_store[par])
        vdummy(2)
        V(lambda e, par=par: e.tensor_add(
            gbufF[:, par * 2048:(par + 1) * 2048], xrl[:, :], ringF[:, :]))
        ov = vfence()
        prev_ov = ov
        b.wait("sync", "SV", ov)
        slot_store[par] = b.dma("sync", out_hbm[t * 128:(t + 1) * 128, :],
                                gbufF[:, par * 2048:(par + 1) * 2048],
                                "SO0" if par == 0 else "SO1")
    for sname in ("XI", "XA", "XB", "XR", "LT0", "LT1", "L2T", "KQ0", "KQ1",
                  "SP", "SO0", "SO1"):
        b.wait("sync", sname)
    for sname in [f"W{i}" for i in range(16)] + [f"AC{t}_{dq}" for t in range(8) for dq in range(4)]:
        b.wait("gpsimd", sname)

    nc.detect_race_conditions = False
    with nc.Block() as block:
        @block.sync
        def _(e):
            for fn in b.q["sync"]:
                fn(e)

        @block.gpsimd
        def _(e):
            for fn in b.q["gpsimd"]:
                fn(e)

        @block.tensor
        def _(e):
            for fn in b.q["tensor"]:
                fn(e)

        @block.vector
        def _(e):
            for fn in b.q["vector"]:
                fn(e)

        @block.scalar
        def _(e):
            for fn in b.q["scalar"]:
                fn(e)

    return nc


_NC_CACHE = {}


def _get_nc():
    if "nc" not in _NC_CACHE:
        _NC_CACHE["nc"] = build_program()
    return _NC_CACHE["nc"]


_TBL_CACHE = {}


def _host_tbl(half):
    if half in _TBL_CACHE:
        return _TBL_CACHE[half]
    import jax
    import jax.numpy as jnp
    cpu = jax.devices("cpu")[0]
    with jax.default_device(cpu):
        inv_freq = 1.0 / 10000.0 ** (jnp.arange(0, 128, 2, dtype=jnp.float32) / 128)
        pos = jnp.arange(2048, dtype=jnp.float32)
        freqs = pos[:, None] * inv_freq[None, :]
        cos = np.asarray(jnp.cos(freqs))
        sin = np.asarray(jnp.sin(freqs))
        lam = jnp.exp2(-8.0 * jnp.arange(1, 17, dtype=jnp.float32) / 16.0)
        r = jnp.arange(64, dtype=jnp.float32)
        diff = r[:, None] - r[None, :]
        diag = np.asarray(jnp.where(diff >= 0, jnp.exp(-lam[:, None, None] * diff), 0.0))
        qdec = np.asarray(jnp.exp(-lam[:, None] * (r + 1.0)))
        kdec = np.asarray(jnp.exp(-lam[:, None] * (64.0 - 1.0 - r)))
        blk = np.asarray(jnp.exp(-lam * 64.0))
    tblv = np.zeros((128, 6176), np.float32)
    p = np.arange(128)
    for t in range(8):
        own = half * 1024 + t * 128 + p
        pre = t * 128 + p
        tblv[:, t * 128:t * 128 + 64] = cos[own]
        tblv[:, t * 128 + 64:t * 128 + 128] = cos[own]
        tblv[:, 1024 + t * 128:1024 + t * 128 + 64] = sin[own]
        tblv[:, 1024 + t * 128 + 64:1024 + t * 128 + 128] = sin[own]
        tblv[:, 2048 + t * 128:2048 + t * 128 + 64] = cos[pre]
        tblv[:, 2048 + t * 128 + 64:2048 + t * 128 + 128] = cos[pre]
        tblv[:, 3072 + t * 128:3072 + t * 128 + 64] = sin[pre]
        tblv[:, 3072 + t * 128 + 64:3072 + t * 128 + 128] = sin[pre]
    for h in range(16):
        tblv[0:64, 4096 + h * 64:4096 + (h + 1) * 64] = diag[h].T
        tblv[64:128, 4096 + h * 64:4096 + (h + 1) * 64] = diag[h].T
        tblv[:, 5120 + h * 64:5120 + (h + 1) * 64] = qdec[h][None, :]
        tblv[:, 6144 + h] = kdec[h][p % 64]
        tblv[:, 6160 + h] = blk[h]
    tblv = np.ascontiguousarray(tblv)
    _TBL_CACHE[half] = tblv
    return tblv


def kernel(L0_attn_x, L0_attn_wq, L0_attn_wk, L0_attn_wv, L0_attn_wo,
           L0_ffn_w_gate, L0_ffn_w_up, L0_ffn_w_down):
    x = np.ascontiguousarray(np.asarray(L0_attn_x, dtype=np.float32))
    wq = np.asarray(L0_attn_wq, dtype=np.float32)
    wk = np.asarray(L0_attn_wk, dtype=np.float32)
    wv = np.asarray(L0_attn_wv, dtype=np.float32)
    wo = np.asarray(L0_attn_wo, dtype=np.float32)
    wg = np.asarray(L0_ffn_w_gate, dtype=np.float32)
    wu = np.asarray(L0_ffn_w_up, dtype=np.float32)
    wd = np.asarray(L0_ffn_w_down, dtype=np.float32)

    wqT = np.ascontiguousarray(wq.T)
    wkT = np.ascontiguousarray(wk.T)
    wvT = np.ascontiguousarray(wv.T)
    woT = np.ascontiguousarray(wo.T)
    wgP = np.ascontiguousarray(
        wg.reshape(64, 128, 16, 128).transpose(0, 3, 2, 1).reshape(64, 128, 2048))
    wuP = np.ascontiguousarray(
        wu.reshape(64, 128, 16, 128).transpose(0, 3, 2, 1).reshape(64, 128, 2048))
    wdT = np.ascontiguousarray(wd.T)

    zeros = np.zeros((T, D), np.float32)
    in_maps = []
    for core in range(8):
        bb, half = core // 2, core % 2
        xs = np.ascontiguousarray(x[bb, half * T:(half + 1) * T, :])
        xp = np.ascontiguousarray(x[bb, 0:T, :]) if half == 1 else zeros
        in_maps.append(dict(
            x_own=xs, x_pre=xp, wqT=wqT, wkT=wkT, wvT=wvT, woT=woT,
            wgP=wgP, wuP=wuP, wdT=wdT, tbl=_host_tbl(half)))

    nc = _get_nc()
    res = run_bass_kernel_spmd(nc, in_maps, core_ids=list(range(8)))
    out = np.empty((B, S, D), np.float32)
    for core in range(8):
        bb, half = core // 2, core % 2
        out[bb, half * T:(half + 1) * T, :] = res.results[core]["out"]
    return out


# version 12 (final)
# speedup vs baseline: 1.1497x; 1.0275x over previous
import numpy as np
import concourse.bass as bass
import concourse.mybir as mybir
from concourse.bass_utils import run_bass_kernel_spmd

F32 = mybir.dt.float32
BF16 = mybir.dt.bfloat16
AOT = mybir.AluOpType
AFT = mybir.ActivationFunctionType
AXX = mybir.AxisListType.X

B, S, D = 4, 2048, 2048
HID = 8192
T, NT, NKC, NCH = 1024, 8, 16, 8
EPS = 1e-5

ENGS = ("sync", "gpsimd", "tensor", "vector", "scalar")
DEBUG = False
DEBUG_BARRIER = True
BARRIERS = set()


class Bld:
    def __init__(self, nc):
        self.nc = nc
        self.q = {e: [] for e in ENGS}
        self.sems = {}
        self.cnt = {}
        names = ["SX", "SW", "ST", "SV", "SS", "SG", "TB", "XI", "XA", "XB", "XC",
                 "XD", "XR", "LT0", "LT1", "LT2", "LT3", "L2T", "KQ0", "KQ1",
                 "SP", "SO0", "SO1", "AC"]
        names += [f"W{i}" for i in range(16)]
        names += [f"AC{t}_{dq}" for t in range(8) for dq in range(4)]
        for s in names:
            self.sems[s] = nc.alloc_semaphore(s)
            self.cnt[s] = 0

    def emit(self, eng, fn):
        self.q[eng].append(fn)

    def wait(self, eng, sem, val=None):
        v = self.cnt[sem] if val is None else val
        if v <= 0:
            return
        s = self.sems[sem]
        self.emit(eng, lambda e, s=s, v=v: e.wait_ge(s, v))

    def op(self, eng, fn, inc=None, amt=1):
        if inc is None:
            self.emit(eng, fn)
            return None
        s = self.sems[inc]
        self.cnt[inc] += amt
        self.emit(eng, lambda e, s=s, a=amt, f=fn: f(e).then_inc(s, a))
        return self.cnt[inc]

    def dma(self, eng, out_ap, in_ap, inc="SW", accum=False):
        s = self.sems[inc]
        self.cnt[inc] += 16
        if accum:
            self.emit(eng, lambda e, o=out_ap, i=in_ap, s=s:
                      e.dma_start(o, i, accum_op=AOT.add).then_inc(s, 16))
        else:
            self.emit(eng, lambda e, o=out_ap, i=in_ap, s=s:
                      e.dma_start(o, i).then_inc(s, 16))
        return self.cnt[inc]

    def dmat(self, out_ap, in_ap, inc="SX"):
        s = self.sems[inc]
        self.cnt[inc] += 16
        self.emit("sync", lambda e, o=out_ap, i=in_ap, s=s:
                  e.dma_start_transpose(o, i).then_inc(s, 16))
        return self.cnt[inc]


def AP(t, off, dims):
    h = t.tensor if hasattr(t, "tensor") else t
    return bass.AP(h, off, dims)


def build_program():
    nc = bass.Bass("TRN2", target_bir_lowering=False)

    x_own = nc.dram_tensor("x_own", [T, D], F32, kind="ExternalInput").ap()
    x_pre = nc.dram_tensor("x_pre", [T, D], F32, kind="ExternalInput").ap()
    wqT = nc.dram_tensor("wqT", [D, 2048], F32, kind="ExternalInput").ap()
    wkT = nc.dram_tensor("wkT", [D, 512], F32, kind="ExternalInput").ap()
    wvT = nc.dram_tensor("wvT", [D, 512], F32, kind="ExternalInput").ap()
    woT = nc.dram_tensor("woT", [2048, D], F32, kind="ExternalInput").ap()
    wgP = nc.dram_tensor("wgP", [64, 128, 2048], F32, kind="ExternalInput").ap()
    wuP = nc.dram_tensor("wuP", [64, 128, 2048], F32, kind="ExternalInput").ap()
    wdT = nc.dram_tensor("wdT", [HID, D], F32, kind="ExternalInput").ap()
    tbl = nc.dram_tensor("tbl", [128, 6176], F32, kind="ExternalInput").ap()
    out_hbm = nc.dram_tensor("out", [T, D], F32, kind="ExternalOutput").ap()
    x1_hbm = nc.dram_tensor("x1spill", [T, D], F32, kind="Internal").ap()

    SB = 16640
    _A = nc.alloc_sbuf_tensor_at
    A = lambda name, shape, dt, offset: _A(name, shape, dt, offset=SB + offset)
    fring = A("fring", [128, 4096], F32, offset=0)
    lnT = A("lnT", [128, 16384], BF16, offset=16384)
    gbuf = A("gbuf", [128, 16384], BF16, offset=49152)
    gbufF = A("gbufF", [128, 4096], F32, offset=49152)
    tgF = A("tgF", [128, 4096], F32, offset=65536)
    qoT = A("qoT", [128, 16384], BF16, offset=81920)
    kT = A("kT", [128, 4096], BF16, offset=114688)
    ktok = A("ktok", [128, 4096], BF16, offset=122880)
    vtok = A("vtok", [128, 4096], BF16, offset=131072)
    KVa = A("KVa", [128, 2048], F32, offset=139264)
    KVb = A("KVb", [128, 2048], F32, offset=147456)
    acc = A("acc", [128, 16384], F32, offset=81920)
    aoF1 = A("aoF1", [128, 6144], F32, offset=114688)
    stage2 = A("stage2", [128, 2048], BF16, offset=139264)
    xrl = A("xrl", [128, 2048], F32, offset=147456)
    KVbf = A("KVbf", [128, 2048], BF16, offset=155648)
    warena = A("warena", [128, 8192], BF16, offset=159744)
    ring = A("ring", [128, 4096], BF16, offset=176128)
    ringF = A("ringF", [128, 2048], F32, offset=176128)
    scos = A("scos", [128, 1024], BF16, offset=184320)
    ssin = A("ssin", [128, 1024], BF16, offset=186368)
    sm0 = 188416
    stat = A("stat", [128, 64], F32, offset=sm0)
    lamF = A("lamF", [128, 80], F32, offset=sm0 + 256)
    idxS = A("idxS", [128, 256], F32, offset=sm0 + 576)
    ktil = A("ktil", [128, 512], BF16, offset=sm0 + 1600)
    qtil = A("qtil", [128, 512], BF16, offset=sm0 + 2624)
    scm = A("scm", [128, 1024], BF16, offset=sm0 + 3648)
    dumV = A("dumV", [128, 16], F32, offset=sm0 + 5696)
    sqd = A("sqd", [128, 2048], BF16, offset=sm0 + 5888)
    scm2 = A("scm2", [128, 1024], BF16, offset=sm0 + 9984)
    dumB = A("dumB", [128, 128], F32, offset=sm0 + 12032)
    ktil2 = A("ktil2", [128, 512], BF16, offset=sm0 + 12544)
    qtil2 = A("qtil2", [128, 512], BF16, offset=sm0 + 13568)

    p0 = nc.alloc_psum_tensor("p0", [128, 2048], F32)
    p1 = nc.alloc_psum_tensor("p1", [128, 2048], F32)
    PS = (p0, p1)

    b = Bld(nc)

    def V(fn, inc=None):
        return b.op("vector", fn, inc=inc)

    def Sc(fn, inc=None):
        return b.op("scalar", fn, inc=inc)

    def Tt(fn, inc=None):
        return b.op("tensor", fn, inc=inc)

    def vdummy(n=2):
        for _ in range(n):
            V(lambda e: e.tensor_copy(dumB[:, 0:64], dumB[:, 64:128]))

    def vfence(inc="SV"):
        vdummy(2)
        return V(lambda e: e.tensor_copy(dumB[:, 0:64], dumB[:, 64:128]), inc=inc)

    def sfence(inc="SS"):
        Sc(lambda e: e.copy(dumV[:, 0:1], dumV[:, 4:5]))
        return Sc(lambda e: e.copy(dumV[:, 1:2], dumV[:, 5:6]), inc=inc)

    dbg_outs = []

    def _dbg(name, src_ap, shape):
        if not DEBUG:
            return
        d = nc.dram_tensor("dbg_" + name, list(shape), F32, kind="ExternalOutput").ap()
        for sname in list(b.sems):
            if sname != "SW":
                b.wait("gpsimd", sname)
        b.dma("gpsimd", d[:, :], src_ap, "SW")
        if DEBUG_BARRIER:
            for eng in ENGS:
                b.wait(eng, "SW")
        dbg_outs.append(name)

    def _bar(pid):
        if pid not in BARRIERS:
            return
        for sname in list(b.sems):
            b.wait("gpsimd", sname)
        b.op("gpsimd", lambda e: e.memset(dumV[:, 8:9], 0.0), inc="SG", amt=1)
        for eng in ENGS:
            b.wait(eng, "SW")
            b.wait(eng, "SG")

    SQD = 8192

    def tg(c0, w=1):
        return tgF[:, c0:c0 + w]

    b.op("gpsimd", lambda e: e.memset(KVa[:, :], 0.0), inc="SG", amt=1)
    b.op("gpsimd", lambda e: e.memset(stat[:, :], 0.0), inc="SG", amt=1)
    b.op("gpsimd", lambda e: e.memset(dumV[:, :], 0.0), inc="SG", amt=1)
    b.op("gpsimd", lambda e: e.memset(dumB[:, :], 0.0), inc="SG", amt=1)
    for _eng in ("vector", "scalar", "tensor", "sync"):
        b.wait(_eng, "SG")
    b.dma("gpsimd", ring[:, 0:1024], tbl[:, 4096:5120], "TB")
    b.dma("gpsimd", ring[:, 1024:2048], tbl[:, 5120:6144], "TB")
    b.dma("gpsimd", lamF[:, 32:48], tbl[:, 6144:6160], "TB")
    b.dma("gpsimd", lamF[:, 48:64], tbl[:, 6160:6176], "TB")
    tb1 = b.dma("gpsimd", scos[:, :], tbl[:, 2048:3072], "TB")
    tb1 = b.dma("gpsimd", ssin[:, :], tbl[:, 3072:4096], "TB")
    b.wait("vector", "TB", tb1)

    def ln_stats(src_tile, jc, sq_wait_sv=None):
        V(lambda e, s=src_tile, jc=jc: e.reduce_sum(stat[:, jc:jc + 1], s, AXX))
        if sq_wait_sv is not None:
            b.wait("scalar", "SV", sq_wait_sv)
        ss = Sc(lambda e, s=src_tile, jc=jc: e.activation(
            sqd[:, :], s, AFT.Square, accum_out=stat[:, 8 + jc:9 + jc]),
            inc="SS")
        b.wait("vector", "SS", ss)
        vdummy(2)
        V(lambda e, jc=jc: e.tensor_scalar_mul(stat[:, 16 + jc:17 + jc], stat[:, jc:jc + 1], 1.0 / 2048.0))
        vdummy(2)
        V(lambda e, jc=jc: e.tensor_mul(stat[:, 40 + jc:41 + jc], stat[:, 16 + jc:17 + jc], stat[:, 16 + jc:17 + jc]))
        V(lambda e, jc=jc: e.tensor_scalar_mul(stat[:, 48 + jc:49 + jc], stat[:, 8 + jc:9 + jc], 1.0 / 2048.0))
        vdummy(2)
        V(lambda e, jc=jc: e.tensor_sub(stat[:, 56 + jc:57 + jc], stat[:, 48 + jc:49 + jc], stat[:, 40 + jc:41 + jc]))
        vdummy(2)
        vv = V(lambda e, jc=jc: e.tensor_scalar_add(stat[:, 40 + jc:41 + jc], stat[:, 56 + jc:57 + jc], EPS), inc="SV")
        b.wait("scalar", "SV", vv)
        ss = Sc(lambda e, jc=jc: e.activation(stat[:, 48 + jc:49 + jc], stat[:, 40 + jc:41 + jc], AFT.Sqrt), inc="SS")
        b.wait("vector", "SS", ss)
        V(lambda e, jc=jc: e.reciprocal(stat[:, 24 + jc:25 + jc], stat[:, 48 + jc:49 + jc]))
        vdummy(2)
        V(lambda e, jc=jc: e.tensor_mul(stat[:, 56 + jc:57 + jc], stat[:, 16 + jc:17 + jc], stat[:, 24 + jc:25 + jc]))
        vdummy(2)
        return V(lambda e, jc=jc: e.tensor_scalar_mul(stat[:, 32 + jc:33 + jc], stat[:, 56 + jc:57 + jc], -1.0), inc="SV")

    def ln_apply(src_tile, dst_tile, jc, sv):
        b.wait("scalar", "SV", sv)
        Sc(lambda e, s=src_tile, d=dst_tile, jc=jc: e.activation(
            d, s, AFT.Identity, bias=stat[:, 32 + jc:33 + jc], scale=stat[:, 24 + jc:25 + jc]))
        return sfence()

    ln_prev = {"applies": None}

    def ln_phase(x_src):
        XS = ("XA", "XB", "XC", "XD")
        xslot = (fring[:, 0:2048], fring[:, 2048:4096],
                 tgF[:, 0:2048], tgF[:, 2048:4096])
        applies = {}
        loads = {}
        transposes = {}
        prev = ln_prev["applies"]
        if prev is not None:
            for lts in ("LT0", "LT1", "LT2", "LT3"):
                b.wait("scalar", lts)
        for j in range(4):
            if prev is not None:
                if j == 2:
                    b.wait("sync", "SV")
                b.wait("sync", "SS", prev[4 + j])
            loads[j] = b.dma("sync", xslot[j], x_src[j * 128:(j + 1) * 128, :], XS[j])
        for j in range(NT):
            sl = j % 4
            slot = xslot[sl]
            ltsem = f"LT{sl}"
            b.wait("vector", XS[sl], loads[j])
            b.wait("scalar", XS[sl], loads[j])
            sv = ln_stats(slot, j)
            stg = gbuf[:, sl * 2048:(sl + 1) * 2048]
            if j >= 4:
                b.wait("scalar", ltsem, transposes[j - 4])
            a = ln_apply(slot, stg, j, sv)
            applies[j] = a
            b.wait("sync", "SS", a)
            transposes[j] = b.dmat(
                AP(lnT, j * 128, [[16384, 128], [1024, 16], [1, 128]]), stg,
                inc=ltsem)
            if j + 4 < NT:
                loads[j + 4] = b.dma("sync", xslot[sl],
                                     x_src[(j + 4) * 128:(j + 5) * 128, :], XS[sl])
        ln_prev["applies"] = applies

    wslot_state = {"next": 0}
    slot_free_st = [0] * 16

    def wslot():
        i = wslot_state["next"]
        wslot_state["next"] = (i + 1) % 16
        return i

    def kouter_pass(lhs_fn, w_src_fn, evac_fn, pre_waits=()):
        slots, loads = [], []
        for k in range(NKC):
            i = wslot()
            if slot_free_st[i] > 0:
                b.wait("gpsimd", "ST", slot_free_st[i])
            sl = warena[:, i * 512:(i + 1) * 512]
            slots.append((i, sl))
            loads.append(b.dma("gpsimd", sl, w_src_fn(k), f"W{i}"))
        b.wait("tensor", "SV")
        b.wait("tensor", "SS")
        for lts in ("LT0", "LT1", "LT2", "LT3"):
            b.wait("tensor", lts)
        for sem in pre_waits:
            b.wait("tensor", sem)
        for k in range(NKC):
            b.wait("tensor", f"W{slots[k][0]}", loads[k])
            for t in range(NT):
                reg = PS[t // 4][:, (t % 4) * 512:(t % 4) * 512 + 512]
                lhs = lhs_fn(k, t)
                if t == NT - 1:
                    st = Tt(lambda e, reg=reg, lhs=lhs, sl=slots[k][1], k=k: e.matmul(
                        reg, lhs, sl, start=(k == 0), stop=(k == NKC - 1)), inc="ST")
                    slot_free_st[slots[k][0]] = st
                else:
                    Tt(lambda e, reg=reg, lhs=lhs, sl=slots[k][1], k=k: e.matmul(
                        reg, lhs, sl, start=(k == 0), stop=(k == NKC - 1)))
        stlast = b.cnt["ST"]
        for t in range(NT):
            reg = PS[t // 4][:, (t % 4) * 512:(t % 4) * 512 + 512]
            evac_fn(t, reg, stlast)

    def ln_lhs(k, t):
        return AP(lnT, k * 1024 + t * 128, [[16384, 128], [1, 128]])

    def emit_rope(psrc, dst, t, nh):
        pT, po = psrc.tensor, psrc.offset
        dstT, do = dst.tensor, dst.offset
        cos_lo = AP(scos, t * 128, [[1024, 128], [0, nh], [1, 64]])
        sin_lo = AP(ssin, t * 128, [[1024, 128], [0, nh], [1, 64]])
        q_lo = bass.AP(pT, po, [[2048, 128], [128, nh], [1, 64]])
        q_hi = bass.AP(pT, po + 64, [[2048, 128], [128, nh], [1, 64]])
        o_lo = bass.AP(dstT, do, [[16384, 128], [128, nh], [1, 64]])
        o_hi = bass.AP(dstT, do + 64, [[16384, 128], [128, nh], [1, 64]])
        m1 = AP(tgF, 96, [[4096, 128], [64, nh], [1, 64]])
        m2 = AP(tgF, 608, [[4096, 128], [64, nh], [1, 64]])
        m3 = AP(tgF, 1120, [[4096, 128], [64, nh], [1, 64]])
        m4 = AP(tgF, 1632, [[4096, 128], [64, nh], [1, 64]])
        V(lambda e: e.tensor_mul(m1, q_lo, cos_lo))
        V(lambda e: e.tensor_mul(m2, q_hi, sin_lo))
        V(lambda e: e.tensor_mul(m3, q_hi, cos_lo))
        V(lambda e: e.tensor_mul(m4, q_lo, sin_lo))
        vdummy(2)
        V(lambda e: e.tensor_sub(o_lo, m1, m2))
        vdummy(2)
        V(lambda e: e.tensor_add(o_hi, m3, m4))
        vdummy(4)

    stage_free = {0: 0, 1: 0}

    def evac_k(to_kT):
        def f(t, reg, st):
            b.wait("vector", "ST", st)
            par = t % 2
            stB = gbuf[:, 6144 + par * 1024: 6144 + par * 1024 + 512]
            if stage_free[par] > 0:
                b.wait("vector", "KQ0" if par == 0 else "KQ1", stage_free[par])
            emit_rope(reg, stB, t, 4)
            V(lambda e, t=t, stB=stB: e.tensor_copy(
                AP(ktok, t * 512, [[4096, 128], [1, 512]]), stB))
            kv = vfence()
            if to_kT:
                b.wait("sync", "SV", kv)
                stage_free[par] = b.dmat(
                    AP(kT, t * 128, [[4096, 128], [1024, 4], [1, 128]]), stB,
                    inc="KQ0" if par == 0 else "KQ1")
        return f

    def evac_v(t, reg, st):
        b.wait("vector", "ST", st)
        V(lambda e, reg=reg, t=t: e.tensor_copy(
            AP(vtok, t * 512, [[4096, 128], [1, 512]]), reg),
          inc="SV" if t == NT - 1 else None)

    def evac_q(qq):
        def f(t, reg, st):
            b.wait("vector", "ST", st)
            par = t % 2
            stB = gbuf[:, 6144 + par * 1024: 6144 + par * 1024 + 512]
            if stage_free[par] > 0:
                b.wait("vector", "KQ0" if par == 0 else "KQ1", stage_free[par])
            emit_rope(reg, stB, t, 4)
            qv = vfence()
            b.wait("sync", "SV", qv)
            stage_free[par] = b.dmat(
                AP(qoT, (4 * qq) * 1024 + t * 128, [[16384, 128], [1024, 4], [1, 128]]), stB,
                inc="KQ0" if par == 0 else "KQ1")
        return f

    def kv_scan(with_output):
        groups = [(c64, g) for c64 in range(16) for g in range(4)]
        KT = (ktil, ktil2)
        QT = (qtil, qtil2)
        last_a = [0, 0]
        smm = {}
        vm = {}
        vprep = {}

        def emitA(s):
            c64, g = groups[s]
            par = s % 2
            t, hh = c64 // 2, c64 % 2
            po = hh * 64
            kt_, qt_ = KT[par], QT[par]
            if last_a[par] > 0:
                b.wait("vector", "ST", last_a[par])
            for hl in range(4):
                h = 4 * g + hl
                V(lambda e, t=t, po=po, g=g, hl=hl, h=h, kt_=kt_: e.tensor_scalar(
                    kt_[po:po + 64, hl * 128:(hl + 1) * 128],
                    ktok[po:po + 64, t * 512 + g * 128:t * 512 + (g + 1) * 128],
                    lamF[po:po + 64, 32 + h:33 + h], None, AOT.mult))
            if with_output:
                for hl in range(4):
                    h = 4 * g + hl
                    V(lambda e, t=t, po=po, h=h, hl=hl, qt_=qt_: e.tensor_mul(
                        qt_[:, hl * 64:(hl + 1) * 64],
                        AP(qoT, h * 1024 + t * 128 + po, [[16384, 128], [1, 64]]),
                        ring[:, 1024 + h * 64:1024 + (h + 1) * 64]))
            vprep[s] = V(lambda e: e.tensor_copy(dumV[:, 1:2], dumV[:, 5:6]), inc="SV")
            b.wait("tensor", "SV", vprep[s])
            if with_output:
                pp = PS[par]
                smm[s] = Tt(lambda e, t=t, po=po, g=g, pp=pp: e.matmul(
                    pp[po:po + 64, 1024:1280],
                    kT[:, g * 1024 + t * 128 + po: g * 1024 + t * 128 + po + 64],
                    AP(qoT, (4 * g) * 1024 + t * 128 + po, [[16384, 128], [1024, 4], [1, 64]]),
                    start=True, stop=True), inc="ST")

        def emitB(s):
            c64, g = groups[s]
            par = s % 2
            t, hh = c64 // 2, c64 % 2
            po = hh * 64
            pp = PS[par]
            kt_, qt_ = KT[par], QT[par]
            src = (KVa, KVb)[c64 % 2]
            dst = (KVb, KVa)[c64 % 2]
            if with_output:
                b.wait("vector", "ST", smm[s])
                scm_sl = scm[po:po + 64, par * 256:par * 256 + 256]
                vm[s] = V(lambda e, g=g, po=po, pp=pp, scm_sl=scm_sl: e.tensor_mul(
                    scm_sl, pp[po:po + 64, 1024:1280],
                    ring[po:po + 64, (4 * g) * 64:(4 * g + 4) * 64]), inc="SV")
                b.wait("tensor", "SV", vm[s])
                b.wait("tensor", "SS")
                for hl in range(4):
                    h = 4 * g + hl
                    Tt(lambda e, t=t, po=po, g=g, hl=hl, pp=pp, par=par: e.matmul(
                        pp[:, 1536 + hl * 64:1536 + (hl + 1) * 64],
                        vtok[po:po + 64, t * 512 + g * 128:t * 512 + (g + 1) * 128],
                        scm[po:po + 64, par * 256 + hl * 64:par * 256 + (hl + 1) * 64],
                        start=True, stop=False))
                    Tt(lambda e, h=h, hl=hl, pp=pp, qt_=qt_: e.matmul(
                        pp[:, 1536 + hl * 64:1536 + (hl + 1) * 64],
                        KVbf[:, h * 128:(h + 1) * 128],
                        qt_[:, hl * 64:(hl + 1) * 64],
                        start=False, stop=True))
            for hl in range(4):
                st = Tt(lambda e, t=t, po=po, g=g, hl=hl, pp=pp, kt_=kt_: e.matmul(
                    pp[:, 512 + hl * 128:512 + (hl + 1) * 128],
                    kt_[po:po + 64, hl * 128:(hl + 1) * 128],
                    vtok[po:po + 64, t * 512 + g * 128:t * 512 + (g + 1) * 128],
                    start=True, stop=True),
                    inc="ST" if hl == 3 else None)
            last_a[par] = st
            b.wait("vector", "ST", st)
            if with_output:
                V(lambda e, t=t, po=po, g=g, pp=pp: e.tensor_copy(
                    AP(qoT, (4 * g) * 1024 + t * 128 + po, [[16384, 128], [1024, 4], [1, 64]]),
                    pp[:, 1536:1792]))
            for hl in range(4):
                h = 4 * g + hl
                V(lambda e, h=h, hl=hl, src=src, dst=dst, pp=pp: e.scalar_tensor_tensor(
                    dst[:, h * 128:(h + 1) * 128],
                    src[:, h * 128:(h + 1) * 128],
                    lamF[:, 48 + h:49 + h],
                    pp[:, 512 + hl * 128:512 + (hl + 1) * 128],
                    AOT.mult, AOT.add))
            vu = V(lambda e: e.tensor_copy(dumV[:, 2:3], dumV[:, 6:7]), inc="SV")
            if with_output and c64 < 15:
                b.wait("scalar", "SV", vu)
                for hl in range(4):
                    h = 4 * g + hl
                    Sc(lambda e, h=h, dst=dst: e.copy(
                        KVbf[:, h * 128:(h + 1) * 128], dst[:, h * 128:(h + 1) * 128]),
                        inc="SS" if hl == 3 else None)

        for s in range(len(groups)):
            emitA(s)
            if s >= 1:
                emitB(s - 1)
        emitB(len(groups) - 1)

    _dbg("scos", scos[:, :], (128, 1024))
    _dbg("ssin", ssin[:, :], (128, 1024))
    _dbg("diag", ring[:, 0:2048], (128, 2048))
    _dbg("qdec", ring[:, 2048:4096], (128, 2048))
    _dbg("lam", lamF[:, 0:64], (128, 64))
    ln_phase(x_pre)
    _bar(2)
    _dbg("lnTpre", lnT[:, :], (128, 16384))
    kouter_pass(ln_lhs, lambda k: wkT[k * 128:(k + 1) * 128, 0:512], evac_k(False))
    kouter_pass(ln_lhs, lambda k: wvT[k * 128:(k + 1) * 128, 0:512], evac_v)
    b.wait("tensor", "SV")
    b.wait("vector", "SW")
    _dbg("ktokpre", ktok[:, :], (128, 4096))
    _dbg("vtokpre", vtok[:, :], (128, 4096))
    kv_scan(with_output=False)
    _bar(3)
    _dbg("KV0", KVa[:, :], (128, 2048))

    b.wait("gpsimd", "SV")
    tb2 = b.dma("gpsimd", scos[:, :], tbl[:, 0:1024], "TB")
    tb2 = b.dma("gpsimd", ssin[:, :], tbl[:, 1024:2048], "TB")
    b.wait("vector", "TB", tb2)
    _dbg("scos0", scos[:, :], (128, 1024))
    ln_phase(x_own)
    _bar(4)
    _dbg("lnT1", lnT[:, :], (128, 16384))
    kouter_pass(ln_lhs, lambda k: wkT[k * 128:(k + 1) * 128, 0:512], evac_k(True))
    kouter_pass(ln_lhs, lambda k: wvT[k * 128:(k + 1) * 128, 0:512], evac_v)
    for qq in range(4):
        kouter_pass(ln_lhs,
                    lambda k, qq=qq: wqT[k * 128:(k + 1) * 128, qq * 512:(qq + 1) * 512],
                    evac_q(qq))
    _bar(5)
    _dbg("qT", qoT[:, :], (128, 16384))
    _dbg("kTd", kT[:, :], (128, 4096))
    _dbg("ktok1", ktok[:, :], (128, 4096))
    _dbg("vtok1", vtok[:, :], (128, 4096))

    b.wait("tensor", "KQ0")
    b.wait("tensor", "KQ1")
    b.wait("vector", "KQ0")
    b.wait("vector", "KQ1")
    b.wait("scalar", "SV")
    for h in range(16):
        Sc(lambda e, h=h: e.copy(KVbf[:, h * 128:(h + 1) * 128], KVa[:, h * 128:(h + 1) * 128]),
            inc="SS" if h == 15 else None)
    kv_scan(with_output=True)
    _bar(6)
    _dbg("oT", qoT[:, :], (128, 16384))

    def ao_base(t):
        if t < 3:
            return (aoF1, t * 2048, 6144)
        if t < 5:
            return (fring, (t - 3) * 2048, 4096)
        if t < 7:
            return (gbufF, (t - 5) * 2048, 4096)
        return (tgF, 2048, 4096)

    def ao_tile(t):
        tt, off, pitch = ao_base(t)
        return AP(tt, off, [[pitch, 128], [1, 2048]])

    def ao_slice(t, dq):
        tt, off, pitch = ao_base(t)
        return AP(tt, off + dq * 512, [[pitch, 128], [1, 512]])

    def evac_ao(dq):
        def f(t, reg, st):
            b.wait("vector", "ST", st)
            V(lambda e, reg=reg, t=t, dq=dq: e.tensor_copy(ao_slice(t, dq), reg),
              inc="SV" if t == NT - 1 else None)
        return f

    for dq in range(4):
        kouter_pass(
            lambda k, t: AP(qoT, k * 1024 + t * 128, [[16384, 128], [1, 128]]),
            lambda k, dq=dq: woT[k * 128:(k + 1) * 128, dq * 512:(dq + 1) * 512],
            evac_ao(dq), pre_waits=("KQ0", "KQ1"))

    _bar(7)
    _dbg("ao0", ao_tile(0), (128, 2048))
    _dbg("ao7", ao_tile(7), (128, 2048))
    b.wait("sync", "SV")
    prev_x1v = 0
    prev_store = 0
    for t in range(NT):
        at = ao_tile(t)
        sv = ln_stats(at, t % 8, sq_wait_sv=b.cnt["SV"])
        if prev_x1v > 0:
            b.wait("sync", "SV", prev_x1v)
        xl = b.dma("sync", xrl[:, :], x_own[t * 128:(t + 1) * 128, :], "XR")
        b.wait("vector", "SV", sv)
        V(lambda e, at=at, jc=t % 8: e.tensor_scalar(
            ringF[:, :], at, stat[:, 16 + jc:17 + jc], stat[:, 24 + jc:25 + jc],
            AOT.subtract, AOT.mult))
        b.wait("vector", "XR", xl)
        vdummy(2)
        V(lambda e, at=at: e.tensor_add(at, xrl[:, :], ringF[:, :]))
        x1v = vfence()
        prev_x1v = x1v
        b.wait("sync", "SV", x1v)
        b.dma("sync", x1_hbm[t * 128:(t + 1) * 128, :], at, "SP")
        vdummy(2)
        sv2 = ln_stats(at, t % 8, sq_wait_sv=x1v)
        if prev_store > 0:
            b.wait("scalar", "L2T", prev_store)
        a2 = ln_apply(at, stage2[:, :], t % 8, sv2)
        b.wait("sync", "SS", a2)
        prev_store = b.dmat(AP(lnT, t * 128, [[16384, 128], [1024, 16], [1, 128]]), stage2[:, :],
                            inc="L2T")

    _bar(8)
    _dbg("lnT2", lnT[:, :], (128, 16384))
    b.wait("tensor", "L2T")
    b.wait("vector", "SP")
    dstage_free = [None, None, None, None]
    acc_sw = [[0] * 4 for _ in range(NT)]

    for sub in range(4):
        last_gm = 0
        for ht16 in range(16):
            ht = sub * 16 + ht16
            par = ht16 % 2
            pp = PS[par]
            bg, bu = par * 2, par * 2 + 1
            wmax = max(slot_free_st[bg * 4: bg * 4 + 4] + slot_free_st[bu * 4: bu * 4 + 4])
            if wmax > 0:
                b.wait("gpsimd", "ST", wmax)
            ldg = b.dma("gpsimd", warena[:, bg * 2048:(bg + 1) * 2048], wgP[ht, :, :], f"W{bg * 4}")
            ldu = b.dma("gpsimd", warena[:, bu * 2048:(bu + 1) * 2048], wuP[ht, :, :], f"W{bu * 4}")
            b.wait("tensor", "SS")
            b.wait("tensor", "SV")
            b.wait("tensor", f"W{bg * 4}", ldg)
            b.wait("tensor", f"W{bu * 4}", ldu)
            st = None
            for k in range(NKC):
                for half in range(2):
                    lhs_g = warena[:, bg * 2048 + k * 128: bg * 2048 + (k + 1) * 128]
                    lhs_u = warena[:, bu * 2048 + k * 128: bu * 2048 + (k + 1) * 128]
                    rhs = lnT[:, k * 1024 + half * 512: k * 1024 + half * 512 + 512]
                    Tt(lambda e, pp=pp, half=half, lhs_g=lhs_g, rhs=rhs, k=k: e.matmul(
                        pp[:, half * 512:half * 512 + 512], lhs_g, rhs,
                        start=(k == 0), stop=(k == NKC - 1)))
                    st = Tt(lambda e, pp=pp, half=half, lhs_u=lhs_u, rhs=rhs, k=k: e.matmul(
                        pp[:, 1024 + half * 512:1024 + half * 512 + 512], lhs_u, rhs,
                        start=(k == 0), stop=(k == NKC - 1)),
                        inc="ST" if (k == NKC - 1 and half == 1) else None)
            for i in list(range(bg * 4, bg * 4 + 4)) + list(range(bu * 4, bu * 4 + 4)):
                slot_free_st[i] = st
            b.wait("scalar", "ST", st)
            ss = Sc(lambda e, pp=pp: e.activation(scm[:, :], pp[:, 0:1024], AFT.Silu), inc="SS")
            b.wait("vector", "SS", ss)
            last_gm = V(lambda e, pp=pp, ht16=ht16: e.tensor_mul(
                gbuf[:, ht16 * 1024:(ht16 + 1) * 1024], scm[:, :], pp[:, 1024:2048]), inc="SV")
        b.wait("tensor", "SV", last_gm)
        if sub == 0:
            _dbg("g0", gbuf[:, :], (128, 16384))
        for dq in range(4):
            slots, loads = [], []
            for k in range(NKC):
                i = wslot()
                if slot_free_st[i] > 0:
                    b.wait("gpsimd", "ST", slot_free_st[i])
                sl = warena[:, i * 512:(i + 1) * 512]
                slots.append((i, sl))
                loads.append(b.dma("gpsimd", sl,
                                   wdT[sub * 2048 + k * 128: sub * 2048 + (k + 1) * 128,
                                       dq * 512:(dq + 1) * 512], f"W{i}"))
            b.wait("tensor", "SV")
            for k in range(NKC):
                b.wait("tensor", f"W{slots[k][0]}", loads[k])
                for t in range(NT):
                    reg = PS[t // 4][:, (t % 4) * 512:(t % 4) * 512 + 512]
                    lhs = AP(gbuf, k * 1024 + t * 128, [[16384, 128], [1, 128]])
                    if t == NT - 1:
                        st = Tt(lambda e, reg=reg, lhs=lhs, sl=slots[k][1], k=k: e.matmul(
                            reg, lhs, sl, start=(k == 0), stop=(k == NKC - 1)), inc="ST")
                        slot_free_st[slots[k][0]] = st
                    else:
                        Tt(lambda e, reg=reg, lhs=lhs, sl=slots[k][1], k=k: e.matmul(
                            reg, lhs, sl, start=(k == 0), stop=(k == NKC - 1)))
            stlast = b.cnt["ST"]
            b.wait("vector", "ST", stlast)
            for t in range(NT):
                reg = PS[t // 4][:, (t % 4) * 512:(t % 4) * 512 + 512]
                dsl = acc[:, t * 2048 + dq * 512: t * 2048 + dq * 512 + 512]
                if sub == 0:
                    V(lambda e, reg=reg, dsl=dsl: e.tensor_copy(dsl, reg),
                      inc="SV" if t == NT - 1 else None)
                else:
                    spar = t % 4
                    if dstage_free[spar] is not None:
                        b.wait("vector", *dstage_free[spar])
                    stg = ringF[:, spar * 512:spar * 512 + 512]
                    V(lambda e, reg=reg, stg=stg: e.tensor_copy(stg, reg))
                    vv = vfence()
                    b.wait("gpsimd", "SV", vv)
                    acsem = f"AC{t}_{dq}"
                    if acc_sw[t][dq] > 0:
                        b.wait("gpsimd", acsem, acc_sw[t][dq])
                    sw = b.dma("gpsimd", dsl, stg, acsem, accum=True)
                    dstage_free[spar] = (acsem, sw)
                    acc_sw[t][dq] = sw

    _bar(9)
    _dbg("accd", acc[:, :], (128, 16384))
    for t in range(8):
        for dq in range(4):
            b.wait("vector", f"AC{t}_{dq}")
            b.wait("scalar", f"AC{t}_{dq}")
    b.wait("scalar", "SV")
    b.wait("sync", "SP")
    prev_ov = 0
    slot_store = [0, 0]
    for t in range(NT):
        at = acc[:, t * 2048:(t + 1) * 2048]
        sv = ln_stats(at, t % 8, sq_wait_sv=b.cnt["SV"])
        if prev_ov > 0:
            b.wait("sync", "SV", prev_ov)
        xl = b.dma("sync", xrl[:, :], x1_hbm[t * 128:(t + 1) * 128, :], "XR")
        b.wait("vector", "SV", sv)
        V(lambda e, at=at, jc=t % 8: e.tensor_scalar(
            ringF[:, :], at, stat[:, 16 + jc:17 + jc], stat[:, 24 + jc:25 + jc],
            AOT.subtract, AOT.mult))
        b.wait("vector", "XR", xl)
        par = t % 2
        if slot_store[par] > 0:
            b.wait("vector", "SO0" if par == 0 else "SO1", slot_store[par])
        vdummy(2)
        V(lambda e, par=par: e.tensor_add(
            gbufF[:, par * 2048:(par + 1) * 2048], xrl[:, :], ringF[:, :]))
        ov = vfence()
        prev_ov = ov
        b.wait("sync", "SV", ov)
        slot_store[par] = b.dma("sync", out_hbm[t * 128:(t + 1) * 128, :],
                                gbufF[:, par * 2048:(par + 1) * 2048],
                                "SO0" if par == 0 else "SO1")
    for sname in ("XI", "XA", "XB", "XC", "XD", "XR", "LT0", "LT1", "LT2",
                  "LT3", "L2T", "KQ0", "KQ1", "SP", "SO0", "SO1"):
        b.wait("sync", sname)
    for sname in [f"W{i}" for i in range(16)] + [f"AC{t}_{dq}" for t in range(8) for dq in range(4)]:
        b.wait("gpsimd", sname)

    nc.detect_race_conditions = False
    with nc.Block() as block:
        @block.sync
        def _(e):
            for fn in b.q["sync"]:
                fn(e)

        @block.gpsimd
        def _(e):
            for fn in b.q["gpsimd"]:
                fn(e)

        @block.tensor
        def _(e):
            for fn in b.q["tensor"]:
                fn(e)

        @block.vector
        def _(e):
            for fn in b.q["vector"]:
                fn(e)

        @block.scalar
        def _(e):
            for fn in b.q["scalar"]:
                fn(e)

    return nc


_NC_CACHE = {}


def _get_nc():
    if "nc" not in _NC_CACHE:
        _NC_CACHE["nc"] = build_program()
    return _NC_CACHE["nc"]


_TBL_CACHE = {}


def _host_tbl(half):
    if half in _TBL_CACHE:
        return _TBL_CACHE[half]
    import jax
    import jax.numpy as jnp
    cpu = jax.devices("cpu")[0]
    with jax.default_device(cpu):
        inv_freq = 1.0 / 10000.0 ** (jnp.arange(0, 128, 2, dtype=jnp.float32) / 128)
        pos = jnp.arange(2048, dtype=jnp.float32)
        freqs = pos[:, None] * inv_freq[None, :]
        cos = np.asarray(jnp.cos(freqs))
        sin = np.asarray(jnp.sin(freqs))
        lam = jnp.exp2(-8.0 * jnp.arange(1, 17, dtype=jnp.float32) / 16.0)
        r = jnp.arange(64, dtype=jnp.float32)
        diff = r[:, None] - r[None, :]
        diag = np.asarray(jnp.where(diff >= 0, jnp.exp(-lam[:, None, None] * diff), 0.0))
        qdec = np.asarray(jnp.exp(-lam[:, None] * (r + 1.0)))
        kdec = np.asarray(jnp.exp(-lam[:, None] * (64.0 - 1.0 - r)))
        blk = np.asarray(jnp.exp(-lam * 64.0))
    tblv = np.zeros((128, 6176), np.float32)
    p = np.arange(128)
    for t in range(8):
        own = half * 1024 + t * 128 + p
        pre = t * 128 + p
        tblv[:, t * 128:t * 128 + 64] = cos[own]
        tblv[:, t * 128 + 64:t * 128 + 128] = cos[own]
        tblv[:, 1024 + t * 128:1024 + t * 128 + 64] = sin[own]
        tblv[:, 1024 + t * 128 + 64:1024 + t * 128 + 128] = sin[own]
        tblv[:, 2048 + t * 128:2048 + t * 128 + 64] = cos[pre]
        tblv[:, 2048 + t * 128 + 64:2048 + t * 128 + 128] = cos[pre]
        tblv[:, 3072 + t * 128:3072 + t * 128 + 64] = sin[pre]
        tblv[:, 3072 + t * 128 + 64:3072 + t * 128 + 128] = sin[pre]
    for h in range(16):
        tblv[0:64, 4096 + h * 64:4096 + (h + 1) * 64] = diag[h].T
        tblv[64:128, 4096 + h * 64:4096 + (h + 1) * 64] = diag[h].T
        tblv[:, 5120 + h * 64:5120 + (h + 1) * 64] = qdec[h][None, :]
        tblv[:, 6144 + h] = kdec[h][p % 64]
        tblv[:, 6160 + h] = blk[h]
    tblv = np.ascontiguousarray(tblv)
    _TBL_CACHE[half] = tblv
    return tblv


def kernel(L0_attn_x, L0_attn_wq, L0_attn_wk, L0_attn_wv, L0_attn_wo,
           L0_ffn_w_gate, L0_ffn_w_up, L0_ffn_w_down):
    x = np.ascontiguousarray(np.asarray(L0_attn_x, dtype=np.float32))
    wq = np.asarray(L0_attn_wq, dtype=np.float32)
    wk = np.asarray(L0_attn_wk, dtype=np.float32)
    wv = np.asarray(L0_attn_wv, dtype=np.float32)
    wo = np.asarray(L0_attn_wo, dtype=np.float32)
    wg = np.asarray(L0_ffn_w_gate, dtype=np.float32)
    wu = np.asarray(L0_ffn_w_up, dtype=np.float32)
    wd = np.asarray(L0_ffn_w_down, dtype=np.float32)

    wqT = np.ascontiguousarray(wq.T)
    wkT = np.ascontiguousarray(wk.T)
    wvT = np.ascontiguousarray(wv.T)
    woT = np.ascontiguousarray(wo.T)
    wgP = np.ascontiguousarray(
        wg.reshape(64, 128, 16, 128).transpose(0, 3, 2, 1).reshape(64, 128, 2048))
    wuP = np.ascontiguousarray(
        wu.reshape(64, 128, 16, 128).transpose(0, 3, 2, 1).reshape(64, 128, 2048))
    wdT = np.ascontiguousarray(wd.T)

    zeros = np.zeros((T, D), np.float32)
    in_maps = []
    for core in range(8):
        bb, half = core // 2, core % 2
        xs = np.ascontiguousarray(x[bb, half * T:(half + 1) * T, :])
        xp = np.ascontiguousarray(x[bb, 0:T, :]) if half == 1 else zeros
        in_maps.append(dict(
            x_own=xs, x_pre=xp, wqT=wqT, wkT=wkT, wvT=wvT, woT=woT,
            wgP=wgP, wuP=wuP, wdT=wdT, tbl=_host_tbl(half)))

    nc = _get_nc()
    res = run_bass_kernel_spmd(nc, in_maps, core_ids=list(range(8)))
    out = np.empty((B, S, D), np.float32)
    for core in range(8):
        bb, half = core // 2, core % 2
        out[bb, half * T:(half + 1) * T, :] = res.results[core]["out"]
    return out
